# version 19
# baseline (speedup 1.0000x reference)
import numpy as np
import ml_dtypes

import concourse.bass as bass
import concourse.mybir as mybir
from concourse.bass_utils import run_bass_kernel_spmd

F32 = mybir.dt.float32
BF16 = mybir.dt.bfloat16
AF = mybir.ActivationFunctionType
ALU = mybir.AluOpType
AX = mybir.AxisListType

N_CORES = 8
B_CORE = 4
L = 2048
D = 1024
NH = 16
HD = 64
I_FF = 2048
NE = 8
EPS_LN = 1e-5
EPS_ATT = 1e-4
BIG = 65536.0

N_GROUP = 8
G = 1024
T = 512
N_SUB = 8
KC = 8
NRING = 6


class Prog:
    def __init__(self):
        self.steps = []
        self.cnt = {}

    def n(self, sem):
        return self.cnt.get(sem, 0)

    def after(self, sem):
        return (sem, self.n(sem))

    def step(self, eng, fn=None, waits=(), inc=None):
        ws = []
        for (s, v) in waits:
            assert v <= self.n(s), f"wait on future count {s}={v} > {self.n(s)}"
            if v > 0:
                ws.append((s, v))
        if inc is not None:
            assert fn is not None
            self.cnt[inc[0]] = self.cnt.get(inc[0], 0) + inc[1]
        self.steps.append((eng, fn, tuple(ws), inc))


_TAPS = {}


def build_program():
    nc = bass.Bass("TRN2", target_bir_lowering=False, debug=False)

    x_in = nc.dram_tensor("x", [B_CORE, L, D], F32, kind="ExternalInput").ap()
    wqkv = nc.dram_tensor("wqkv", [D, 3 * D], F32, kind="ExternalInput").ap()
    bqkv = nc.dram_tensor("bqkv", [1, 3 * D], F32, kind="ExternalInput").ap()
    wo_d = nc.dram_tensor("wo", [D, D], F32, kind="ExternalInput").ap()
    bo_d = nc.dram_tensor("bo", [1, D], F32, kind="ExternalInput").ap()
    sgT = nc.dram_tensor("sgT", [D, I_FF], F32, kind="ExternalInput").ap()
    suT = nc.dram_tensor("suT", [D, I_FF], F32, kind="ExternalInput").ap()
    sdT = nc.dram_tensor("sdT", [I_FF, D], F32, kind="ExternalInput").ap()
    gw_d = nc.dram_tensor("gw", [NE, D, I_FF], F32, kind="ExternalInput").ap()
    uw_d = nc.dram_tensor("uw", [NE, D, I_FF], F32, kind="ExternalInput").ap()
    dw_d = nc.dram_tensor("dw", [NE, I_FF, D], F32, kind="ExternalInput").ap()
    rwT_d = nc.dram_tensor("rwT", [D, NE], F32, kind="ExternalInput").ap()
    rb_d = nc.dram_tensor("rbias", [1, NE], F32, kind="ExternalInput").ap()
    id_d = nc.dram_tensor("identc", [128, 128], F32, kind="ExternalInput").ap()
    ctab_d = nc.dram_tensor("costab", [128, L], F32, kind="ExternalInput").ap()
    stab_d = nc.dram_tensor("sintab", [128, L], F32, kind="ExternalInput").ap()
    ccol_d = nc.dram_tensor("coscol", [128, 16], F32, kind="ExternalInput").ap()
    scol_d = nc.dram_tensor("sincol", [128, 16], F32, kind="ExternalInput").ap()
    idx_d = nc.dram_tensor("idx8", [128, NE], F32, kind="ExternalInput").ap()
    idxb_d = nc.dram_tensor("idxb8", [128, NE], F32, kind="ExternalInput").ap()
    out_d = nc.dram_tensor("out", [B_CORE, L, D], F32, kind="ExternalOutput").ap()
    tap_d = {}
    for tname, (shape, dt) in _TAPS.items():
        tap_d[tname] = nc.dram_tensor("tap_" + tname, shape, dt, kind="ExternalOutput").ap()

    A = nc.alloc_sbuf_tensor
    xnf = A("xnf", [128, KC, G], BF16)
    _acc_off = nc._sbuf_addr_for_side(None)
    acc = A("acc", [128, KC, G], F32)
    kcs2 = nc.alloc_sbuf_tensor_at("kcs2s", [128, 2, NH, 2, HD], BF16, offset=_acc_off)
    v2 = nc.alloc_sbuf_tensor_at("v2s", [128, 2, D], BF16, offset=_acc_off + 8192)
    _ae_off = nc._sbuf_addr_for_side(None)
    a_e = A("a_e", [128, KC, G], BF16)
    xnfa = nc.alloc_sbuf_tensor_at("xnfas", [128, 2, KC, 128], BF16, offset=_ae_off)
    kvst = nc.alloc_sbuf_tensor_at("kvst", [128, NH, HD + 1], BF16, offset=_ae_off + 10240)
    wring = A("wring", [128, NRING, 4096], BF16)
    costab = A("costabs", [128, G], BF16)
    sintab = A("sintabs", [128, G], BF16)
    coscol = A("coscols", [128, 16], F32)
    sincol = A("sincols", [128, 16], F32)
    ident = A("idents", [128, 128], F32)
    kvsb = A("kvsbs", [128, B_CORE, NH, 2, HD + 1], BF16)
    wbc = A("wbcs", [128, 2, G], F32)
    xstage = A("xstage", [128, 2, D], F32)
    outst = A("outsts", [128, 2, D], F32)
    gsb = A("gsbs", [128, 2, T], BF16)
    usb = A("usbs", [128, 2, T], BF16)
    rw_sb = A("rw_sbs", [128, KC, NE], F32)
    rb_sb = A("rb_sbs", [1, NE], F32)
    idx8 = A("idx8s", [128, NE], F32)
    idxb8 = A("idxb8s", [128, NE], F32)
    probs = A("probss", [128, N_SUB, NE], F32)
    rtmp = A("rtmps", [128, N_SUB, NE], F32)
    rtmp2 = A("rtmp2s", [128, N_SUB, NE], F32)
    rm1 = A("rm1s", [128, N_SUB], F32)
    rm2 = A("rm2s", [128, N_SUB], F32)
    ri_ = A("ris", [128, N_SUB], F32)
    rmask1 = A("rmask1s", [128, N_SUB, NE], F32)
    rmask2 = A("rmask2s", [128, N_SUB, NE], F32)
    rden = A("rdens", [128, N_SUB], F32)
    w_tm = A("w_tms", [128, N_SUB, NE], F32)
    wrow = A("wrows", [1, 1, G], F32)
    ones_b = A("ones_b", [1, T], BF16)
    zrow = A("zrows", [1, T], BF16)
    ones_f = A("ones_f", [1, 128], F32)
    onesc = A("onescs", [128, 1], BF16)
    epsln = A("epslns", [128, 1], F32)
    bq_sb = A("bq_sbs", [1, D], BF16)
    bo_sb = A("bo_sbs", [1, D], BF16)
    bnst = A("bnsts", [128, 12], F32)
    bnmv = A("bnmvs", [128, 2], F32)
    lnsq = A("lnsqs", [128, 1], F32)
    lnr = A("lnrs", [128, 1], F32)
    xn_tm = A("xn_tms", [128, 2, D], F32)
    xnf32 = A("xnf32s", [128, D], F32)
    qrel = A("qrels", [128, 2, T], BF16)
    qcs4 = A("qcs4s", [128, 3, 2, T], BF16)
    ctxsb = A("ctxsbs", [128, 2, T], BF16)
    nrow4 = A("nrow4s", [1, 2, 2, T], BF16)
    apre = A("apres", [128, KC, T], BF16)

    pb = [nc.alloc_psum_tensor(f"pb{i}", [128, 512], F32) for i in range(8)]

    sems = {s: nc.alloc_semaphore(s) for s in
            ("sWC", "sXC", "sKV", "sTAB", "sXA", "sXR", "sO0", "sO1",
             "sPE", "sVE", "sACT")}
    for i in range(NRING):
        sems[f"sR{i}"] = nc.alloc_semaphore(f"sR{i}")

    p = Prog()
    out_cnt = [0]
    b1_next = [None]

    def dma(q, dst, src, sem, waits=()):
        p.step(q, lambda e, d=dst, s=src: e.dma_start(d, s), waits=waits, inc=(sem, 16))

    def tap(tname, src_ap):
        if tname in tap_d:
            p.step("sync", lambda e, d=tap_d[tname], s=src_ap: e.dma_start(d, s),
                   waits=[p.after("sPE"), p.after("sVE"), p.after("sACT"), p.after("sKV")],
                   inc=("sO0", 16))
            out_cnt[0] += 16

    dma("gpsimd", idx8[:], idx_d, "sWC")
    dma("gpsimd", idxb8[:], idxb_d, "sWC")
    dma("gpsimd", bq_sb[:], bqkv[0:1, 0:D], "sWC")
    dma("gpsimd", wring[0:1, 4, 0:2048], bqkv[0:1, D:3 * D], "sWC")
    dma("gpsimd", bo_sb[:], bo_d, "sWC")
    dma("gpsimd", coscol[:], ccol_d, "sWC")
    dma("gpsimd", sincol[:], scol_d, "sWC")
    w_const = p.after("sWC")
    dma("sync", ident[:], id_d, "sXC")
    dma("sync", rb_sb[:], rb_d, "sXC")
    dma("sync", rw_sb[:], rwT_d.rearrange("(c k) e -> k c e", k=128), "sXC")
    x_const = p.after("sXC")
    p.step("vector", lambda e: e.memset(ones_b[:], 1.0), inc=("sVE", 1))
    p.step("vector", lambda e: e.memset(zrow[:], 0.0), inc=("sVE", 1))
    p.step("vector", lambda e: e.memset(ones_f[:], 1.0), inc=("sVE", 1))
    p.step("vector", lambda e: e.memset(onesc[:], 1.0), inc=("sVE", 1))
    p.step("vector", lambda e: e.memset(epsln[:], EPS_LN), inc=("sVE", 1))
    const_ve = p.after("sVE")

    ring_free = [None] * NRING
    ring_cnt = [0]
    ring_reserved = set()
    ring_perm = set()

    def ring_load(src_ap, kdim=KC, slot=None):
        if slot is None:
            while True:
                b = ring_cnt[0] % NRING
                ring_cnt[0] += 1
                if b not in ring_reserved:
                    break
        else:
            b = slot
            ring_reserved.add(b)
        waits = [ring_free[b]] if ring_free[b] is not None else []
        p.step("gpsimd",
               lambda e, b=b, s=src_ap, kdim=kdim: e.dma_start(
                   wring[:, b].rearrange("p (k i) -> p k i", k=kdim), s),
               waits=waits, inc=(f"sR{b}", 16))
        return b, p.after(f"sR{b}")

    def ring_done(b):
        ring_free[b] = p.after("sPE")
        if b not in ring_perm:
            ring_reserved.discard(b)

    def qa_view(b):
        return wring[:, b].rearrange("p (k i) -> p k i", k=KC)

    def qd_view(b):
        return wring[:, b].rearrange("p (k i) -> p k i", k=4)

    def q_of(w2d, c0):
        return w2d[:, c0:c0 + 512].rearrange("(c k) i -> k c i", k=128)

    ln_rd = {0: [], 1: []}
    ln_ld = {0: None, 1: None}

    def emit_x_dma(x_src, xslot):
        dma("sync", xstage[:, xslot], x_src, "sXA", waits=ln_rd[xslot])
        ln_rd[xslot] = []
        ln_ld[xslot] = p.after("sXA")

    def emit_ln_vec(xslot):
        ld = ln_ld[xslot]
        p.step("vector", lambda e, xslot=xslot: e.bn_stats(bnst[:, 0:6], xstage[:, xslot, 0:512]),
               waits=[ld, const_ve])
        p.step("vector", lambda e, xslot=xslot: e.bn_stats(bnst[:, 6:12], xstage[:, xslot, 512:1024]),
               inc=("sVE", 1))
        p.step("vector", lambda e: e.bn_aggr(bnmv[:], bnst[:]),
               waits=[p.after("sVE"), p.after("sACT")], inc=("sVE", 1))
        p.step("scalar", lambda e: e.activation(lnsq[:], bnmv[:, 1:2], AF.Sqrt, bias=epsln[:]),
               waits=[p.after("sVE")], inc=("sACT", 1))
        p.step("vector", lambda e: e.reciprocal(lnr[:], lnsq[:]),
               waits=[p.after("sACT")], inc=("sVE", 1))
        p.step("vector",
               lambda e, xslot=xslot: e.tensor_scalar(
                   xn_tm[:, xslot], xstage[:, xslot], bnmv[:, 0:1], lnr[:], ALU.subtract, ALU.mult),
               waits=[p.after("sVE")] + xn_gate[xslot], inc=("sVE", 1))
        ln_rd[xslot] = [p.after("sVE")]
        ln_done[xslot] = p.after("sVE")
        return ln_done[xslot]

    bank_rd = {i: [] for i in range(8)}
    xn_gate = {0: [], 1: []}
    ln_done = {0: None, 1: None}

    def emit_transposes(xslot, dst_bf_slices, dst_f32=None, banks=(5, 7)):
        xn_ready = ln_done[xslot]
        for rnd in range(4):
            bk = banks[rnd % 2]
            ev = bank_rd[bk]
            for j in range(2):
                c = rnd * 2 + j
                w = []
                if j == 0:
                    w = [xn_ready, x_const] + ev
                    if rnd == 0:
                        w.append(p.after("sACT"))
                p.step("tensor",
                       lambda e, c=c, j=j, bk=bk, xslot=xslot: e.transpose(
                           pb[bk][:, j * 128:(j + 1) * 128],
                           xn_tm[:, xslot, c * 128:(c + 1) * 128], ident[:]),
                       waits=w, inc=(("sPE", 1) if j == 1 else None))
            bank_rd[bk] = []
            tp = p.after("sPE")
            p.step("scalar", lambda e, d=dst_bf_slices[rnd], bk=bk: e.copy(
                       d, pb[bk][:, 0:256].rearrange("p (c t) -> p c t", t=128)),
                   waits=[tp], inc=("sACT", 1))
            if dst_f32 is not None:
                p.step("scalar",
                       lambda e, rnd=rnd, bk=bk: e.copy(
                           dst_f32[:, rnd * 256:(rnd + 1) * 256], pb[bk][:, 0:256]),
                       inc=("sACT", 1))
            bank_rd[bk] = [p.after("sACT")]
        xn_gate[xslot] = [p.after("sPE")]
        return p.after("sACT")

    kvq = [ring_load(q_of(wqkv, D + qi * 512)) for qi in range(4)]

    tpa_done = {0: [], 1: []}
    kcs_done2 = {}
    v_done2 = {}
    kcs_free = {0: [], 1: []}
    qrel_rd = {0: [], 1: []}

    def tp_round_a(slot, rnd):
        ev = bank_rd[7]
        for j in range(2):
            c = rnd * 2 + j
            w = ([ln_done[slot], x_const] + ev) if j == 0 else []
            p.step("tensor",
                   lambda e, c=c, j=j, slot=slot: e.transpose(
                       pb[7][:, j * 128:(j + 1) * 128],
                       xn_tm[:, slot, c * 128:(c + 1) * 128], ident[:]),
                   waits=w, inc=(("sPE", 1) if j == 1 else None))
        bank_rd[7] = []
        tp = p.after("sPE")
        p.step("scalar",
               lambda e, rnd=rnd, slot=slot: e.copy(
                   xnfa[:, slot, rnd * 2:(rnd + 1) * 2, :],
                   pb[7][:, 0:256].rearrange("p (c t) -> p c t", t=128)),
               waits=[tp], inc=("sACT", 1))
        bank_rd[7] = [p.after("sACT")]
        if rnd == 3:
            tpa_done[slot] = [p.after("sACT")]
            xn_gate[slot] = [p.after("sPE")]

    def emit_kv_a(sub2):
        slot2 = sub2 % 2
        for half in range(2):
            for hh in range(8):
                h = half * 8 + hh
                col = hh * 64
                p.step("tensor",
                       lambda e, h=h, bank=half, col=col, sub2=sub2, slot2=slot2: e.matmul(
                           pb[bank][:, col:col + 64],
                           kcs2[:, slot2, h].rearrange("p a d -> p (a d)"),
                           v2[:, slot2, h * 64:(h + 1) * 64],
                           start=False, stop=(sub2 == 15),
                           skip_group_check=True),
                       waits=([kcs_done2[slot2], v_done2[slot2]] if (half == 0 and hh == 0) else []))
                p.step("tensor",
                       lambda e, h=h, sub2=sub2, slot2=slot2: e.matmul(
                           pb[6][:, h:h + 1],
                           kcs2[:, slot2, h].rearrange("p a d -> p (a d)"), onesc[:],
                           start=False, stop=(sub2 == 15),
                           skip_group_check=True),
                       inc=(("sPE", 1) if (half == 1 and hh == 7) else None))
        kcs_free[slot2] = [p.after("sPE")]

    for s in range(B_CORE):
        for bank in (0, 1, 6):
            p.step("tensor",
                   lambda e, bank=bank: e.matmul(
                       pb[bank][:], ones_b[0:1, 0:128], zrow[:],
                       start=True, stop=True, skip_group_check=True),
                   waits=[const_ve] if s == 0 else [],
                   inc=(("sPE", 1) if bank == 6 else None))
        emit_x_dma(x_in[s, 0:128, :], 0)
        emit_x_dma(x_in[s, 128:256, :], 1)
        emit_ln_vec(0)
        emit_x_dma(x_in[s, 256:384, :], 0)
        emit_ln_vec(1)
        for rnd in range(4):
            tp_round_a(0, rnd)
        for sub in range(16):
            slot = sub % 2
            if sub + 2 < 16:
                emit_ln_vec(slot)
            grp_done = {}
            for grp in range(4):
                bank = 2 + grp
                buf, bufw = kvq[grp]
                for kc in range(KC):
                    p.step("tensor",
                           lambda e, bank=bank, kc=kc, buf=buf, slot=slot: e.matmul(
                               pb[bank][:], xnfa[:, slot, kc, :], qa_view(buf)[:, kc, :],
                               start=(kc == 0), stop=False, skip_group_check=True),
                           waits=(tpa_done[slot] + [bufw] + bank_rd[bank]) if kc == 0 else [])
                bank_rd[bank] = []
                c0 = (grp // 2) * 1024 + (grp % 2) * 512
                p.step("tensor",
                       lambda e, bank=bank, c0=c0: e.matmul(
                           pb[bank][:], ones_b[0:1, 0:128], wring[0:1, 4, c0:c0 + 512],
                           start=False, stop=True, skip_group_check=True),
                       waits=([w_const, const_ve] if (s == 0 and sub == 0) else []),
                       inc=("sPE", 1))
                grp_done[grp] = p.after("sPE")
                if sub + 1 < 16:
                    tp_round_a((sub + 1) % 2, grp)
            for half in range(2):
                p.step("scalar",
                       lambda e, half=half: e.activation(qrel[:, half], pb[2 + half][:], AF.Relu),
                       waits=[grp_done[half]] + qrel_rd[half], inc=("sACT", 1))
                qrel_rd[half] = []
                kr = p.after("sACT")
                bank_rd[2 + half] = [kr]
                p.step("scalar",
                       lambda e, half=half, slot=slot: e.copy(
                           v2[:, slot, half * 512:(half + 1) * 512], pb[4 + half][:]),
                       waits=[grp_done[2 + half]] + kcs_free[slot], inc=("sACT", 1))
                bank_rd[4 + half] = [p.after("sACT")]
                if half == 1:
                    v_done2[slot] = p.after("sACT")
                p.step("vector",
                       lambda e, half=half, sub=sub, slot=slot: e.tensor_scalar(
                           kcs2[:, slot, half * 8:(half + 1) * 8, 0, :],
                           qrel[:, half].rearrange("p (h d) -> p h d", d=HD),
                           coscol[:, sub:sub + 1], None, ALU.mult),
                       waits=[kr, w_const] + kcs_free[slot], inc=("sVE", 1))
                p.step("vector",
                       lambda e, half=half, sub=sub, slot=slot: e.tensor_scalar(
                           kcs2[:, slot, half * 8:(half + 1) * 8, 1, :],
                           qrel[:, half].rearrange("p (h d) -> p h d", d=HD),
                           sincol[:, sub:sub + 1], None, ALU.mult),
                       inc=("sVE", 1))
                qrel_rd[half] = [p.after("sVE")]
            kcs_free[slot] = []
            kcs_done2[slot] = p.after("sVE")
            if sub + 3 < 16:
                emit_x_dma(x_in[s, (sub + 3) * 128:(sub + 4) * 128, :], (sub + 3) % 2)
            if sub >= 1:
                emit_kv_a(sub - 1)
        emit_kv_a(15)
        kv_done = p.after("sPE")
        p.step("scalar",
               lambda e: e.copy(kvst[:, 0:8, 0:HD],
                                pb[0][:].rearrange("p (h d) -> p h d", d=HD)),
               waits=[kv_done, p.after("sKV")], inc=("sACT", 1))
        p.step("scalar",
               lambda e: e.copy(kvst[:, 8:16, 0:HD],
                                pb[1][:].rearrange("p (h d) -> p h d", d=HD)),
               inc=("sACT", 1))
        p.step("scalar", lambda e: e.copy(kvst[:, :, HD], pb[6][:, 0:16]),
               inc=("sACT", 1))
        kv_evac = p.after("sACT")
        kvst_v = kvst.rearrange("p (h e2) d -> p h e2 d", e2=2)
        kvsb_v = kvsb.rearrange("p s (h e2) c d -> p s h e2 c d", e2=2)
        for par in range(2):
            for cs in range(2):
                dma("gpsimd",
                    kvsb_v[par * 64:(par + 1) * 64, s, :, par, cs, :],
                    kvst_v[cs * 64:(cs + 1) * 64, :, par, :],
                    "sKV", waits=[kv_evac])
        kv_moved = p.after("sKV")
        p.step("tensor", None, waits=[kv_evac])
    kv_ready = p.after("sKV")
    for qi in range(4):
        ring_done(kvq[qi][0])
    ring_free[4] = p.after("sPE")
    ring_perm.update({4, 5})
    ring_reserved.update({4, 5})
    for i in range(8):
        bank_rd[i] = [p.after("sACT"), p.after("sVE")]

    tap("kvsb", kvsb[:])
    b1s = {
        "qcs_free": {i: [] for i in range(3)},
        "qrel_free": {i: [] for i in range(2)},
        "ctxsb_free": {i: [] for i in range(2)},
        "nrow_free": {i: [] for i in range(2)},
    }
    for g in range(N_GROUP):
        s, gh2 = divmod(g, 2)
        t0 = gh2 * G

        dma("gpsimd", costab[:], ctab_d[:, t0:t0 + G], "sTAB", waits=[p.after("sVE")])
        dma("gpsimd", sintab[:], stab_d[:, t0:t0 + G], "sTAB")
        tab_ld = p.after("sTAB")

        def emit_x_tp(sub, slot):
            for rnd in range(4):
                bk = (5, 7)[rnd % 2]
                ev = bank_rd[bk]
                for j in range(2):
                    w = ([ln_ld[slot], x_const] + ev) if j == 0 else []
                    p.step("tensor",
                           lambda e, c=rnd * 2 + j, j=j, bk=bk, slot=slot: e.transpose(
                               pb[bk][:, j * 128:(j + 1) * 128],
                               xstage[:, slot, c * 128:(c + 1) * 128], ident[:]),
                           waits=w, inc=(("sPE", 1) if j == 1 else None))
                bank_rd[bk] = []
                tp = p.after("sPE")
                p.step("scalar",
                       lambda e, rnd=rnd, bk=bk, sub=sub: e.copy(
                           acc[:, rnd * 2:(rnd + 1) * 2, sub * 128:sub * 128 + 128],
                           pb[bk][:, 0:256].rearrange("p (c t) -> p c t", t=128)),
                       waits=[tp], inc=("sACT", 1))
                bank_rd[bk] = [p.after("sACT")]
            ln_rd[slot].append(p.after("sPE"))

        if g == 0:
            emit_x_dma(x_in[s, t0:t0 + 128, :], 0)
            emit_x_dma(x_in[s, t0 + 128:t0 + 256, :], 1)
        emit_ln_vec(0)
        for sub in range(N_SUB):
            l0 = t0 + sub * 128
            if sub + 1 < N_SUB:
                emit_ln_vec((sub + 1) % 2)
            emit_transposes(
                sub % 2,
                [xnf[:, r * 2:(r + 1) * 2, sub * 128:sub * 128 + 128] for r in range(4)],
                dst_f32=xnf32)
            xf_done = p.after("sACT")
            emit_x_tp(sub, sub % 2)
            if sub + 2 < N_SUB:
                emit_x_dma(x_in[s, l0 + 256:l0 + 384, :], sub % 2)
            for kc in range(KC):
                p.step("tensor",
                       lambda e, kc=kc, sub=sub: e.matmul(
                           pb[6][:, sub * 8:(sub + 1) * 8],
                           xnf32[:, kc * 128:(kc + 1) * 128], rw_sb[:, kc, :],
                           start=(kc == 0), stop=False, skip_group_check=True),
                       waits=([xf_done, x_const] + bank_rd[6] if kc == 0 else []))
            if sub == 0:
                bank_rd[6] = []
            p.step("tensor",
                   lambda e, sub=sub: e.matmul(
                       pb[6][:, sub * 8:(sub + 1) * 8], ones_f[:], rb_sb[:],
                       start=False, stop=True, skip_group_check=True),
                   waits=[const_ve], inc=("sPE", 1))
        acc_init_done = p.after("sACT")
        rt_done = p.after("sPE")
        p.step("scalar",
               lambda e: e.activation(probs.rearrange("p s e -> p (s e)"), pb[6][:, 0:64], AF.Sigmoid),
               waits=[rt_done, p.after("sVE")], inc=("sACT", 1))
        probs_done = p.after("sACT")
        bank_rd[6] = [probs_done]
        if g == 0:
            tap("xnf_g0", xnf[:])
            tap("probs_g0", probs[:])

        def rstep(fn, waits=()):
            for i in range(N_SUB):
                p.step("vector", (lambda e, i=i: fn(e, i)),
                       waits=(waits if i == 0 else ()),
                       inc=(("sVE", 1) if i == N_SUB - 1 else None))
        rstep(lambda e, i: e.reduce_max(rm1[:, i:i + 1], probs[:, i], axis=AX.X),
              waits=[probs_done, w_const])
        rstep(lambda e, i: e.tensor_scalar(rtmp[:, i], probs[:, i], rm1[:, i:i + 1], None, ALU.is_equal))
        rstep(lambda e, i: e.scalar_tensor_tensor(rtmp[:, i], rtmp[:, i], -BIG, idxb8[:], ALU.mult, ALU.add))
        rstep(lambda e, i: e.tensor_reduce(ri_[:, i:i + 1], rtmp[:, i], axis=AX.X, op=ALU.min))
        rstep(lambda e, i: e.tensor_scalar(rmask1[:, i], idx8[:], ri_[:, i:i + 1], None, ALU.is_equal))
        rstep(lambda e, i: e.scalar_tensor_tensor(rtmp2[:, i], rmask1[:, i], -BIG, probs[:, i], ALU.mult, ALU.add))
        rstep(lambda e, i: e.reduce_max(rm2[:, i:i + 1], rtmp2[:, i], axis=AX.X))
        rstep(lambda e, i: e.tensor_scalar(rtmp[:, i], rtmp2[:, i], rm2[:, i:i + 1], None, ALU.is_equal))
        rstep(lambda e, i: e.scalar_tensor_tensor(rtmp[:, i], rtmp[:, i], -BIG, idxb8[:], ALU.mult, ALU.add))
        rstep(lambda e, i: e.tensor_reduce(ri_[:, i:i + 1], rtmp[:, i], axis=AX.X, op=ALU.min))
        rstep(lambda e, i: e.tensor_scalar(rmask2[:, i], idx8[:], ri_[:, i:i + 1], None, ALU.is_equal))
        rstep(lambda e, i: e.tensor_add(rtmp[:, i], rmask1[:, i], rmask2[:, i]))
        rstep(lambda e, i: e.tensor_add(rden[:, i:i + 1], rm1[:, i:i + 1], rm2[:, i:i + 1]))
        rstep(lambda e, i: e.reciprocal(rm1[:, i:i + 1], rden[:, i:i + 1]))
        rstep(lambda e, i: e.scalar_tensor_tensor(w_tm[:, i], probs[:, i], rm1[:, i:i + 1], rtmp[:, i],
                                                  ALU.mult, ALU.mult))
        wtm_done = p.after("sVE")
        if g == 0:
            tap("wtm_g0", w_tm[:])

        qcs_done = {}
        nrow_done = {}
        ctx_evac = {}
        apre_all = {}
        wout_done = {}
        wq_buf = {}
        wo_buf = {}

        def b1_load_wq(gh):
            wq_buf[gh] = [ring_load(q_of(wqkv, qi * 512), slot=4 + qi) for qi in range(2)]

        def b1_release_wq(gh):
            for bb, _ in wq_buf[gh]:
                ring_done(bb)

        def b1_load_wo(gh):
            wo_buf[gh] = [ring_load(q_of(wo_d, qi * 512), slot=4 + qi) for qi in range(2)]

        def b1_release_wo(gh):
            for bb, _ in wo_buf[gh]:
                ring_done(bb)

        def emit_proj(gh, oc, qb):
            sl4, sl2 = oc % 3, oc % 2
            buf, bufw = wq_buf[gh][oc // 4]
            for kc in range(KC):
                p.step("tensor",
                       lambda e, qb=qb, kc=kc, oc=oc, buf=buf, gh=gh: e.matmul(
                           pb[qb][:], qa_view(buf)[:, kc, (oc % 4) * 128:(oc % 4) * 128 + 128],
                           xnf[:, kc, gh * T:(gh + 1) * T],
                           start=(kc == 0), stop=False, skip_group_check=True),
                       waits=([bufw, kv_ready] + bank_rd[qb]) if kc == 0 else [])
            bank_rd[qb] = []
            p.step("tensor",
                   lambda e, qb=qb, oc=oc: e.matmul(
                       pb[qb][:], bq_sb[0:1, oc * 128:(oc + 1) * 128], ones_b[0:1, :],
                       start=False, stop=True, skip_group_check=True),
                   waits=[w_const], inc=("sPE", 1))
            qmm = p.after("sPE")
            p.step("scalar",
                   lambda e, qb=qb, sl2=sl2: e.activation(qrel[:, sl2], pb[qb][:], AF.Relu),
                   waits=[qmm] + b1s["qrel_free"][sl2], inc=("sACT", 1))
            b1s["qrel_free"][sl2] = []
            bank_rd[qb] = [p.after("sACT")]
            qre = p.after("sACT")
            p.step("vector",
                   lambda e, gh=gh, sl4=sl4, sl2=sl2: e.tensor_mul(
                       qcs4[:, sl4, 0, :], qrel[:, sl2], costab[:, gh * T:(gh + 1) * T]),
                   waits=[qre, tab_ld] + b1s["qcs_free"][sl4], inc=("sVE", 1))
            b1s["qcs_free"][sl4] = []
            p.step("vector",
                   lambda e, gh=gh, sl4=sl4, sl2=sl2: e.tensor_mul(
                       qcs4[:, sl4, 1, :], qrel[:, sl2], sintab[:, gh * T:(gh + 1) * T]),
                   inc=("sVE", 1))
            m = p.after("sVE")
            b1s["qrel_free"][sl2] = [m]
            qcs_done[(gh, oc)] = m

        def emit_ctxa(gh, oc, cb0, cb1):
            for hh in range(2):
                h = 2 * oc + hh
                b = hh * 64
                cb = (cb0, cb1)[hh]
                nb = 6 + hh
                tp_ctx = (b, b) if b else None
                tp_nrm = (b, 0) if b else None
                for cs in range(2):
                    p.step("tensor",
                           lambda e, s=s, h=h, b=b, cs=cs, cb=cb, tp=tp_ctx, oc=oc, gh=gh: e.matmul(
                               pb[cb][b:b + 64, 0:T],
                               kvsb[b:b + 64, s, h, cs, 0:HD], qcs4[b:b + 64, oc % 3, cs, :],
                               start=(cs == 0), stop=(cs == 1),
                               tile_position=tp, skip_group_check=True),
                           waits=([qcs_done[(gh, oc)]] + bank_rd[cb]) if cs == 0 else [])
                bank_rd[cb] = []
                for cs in range(2):
                    p.step("tensor",
                           lambda e, s=s, h=h, b=b, cs=cs, nb=nb, tp=tp_nrm, oc=oc: e.matmul(
                               pb[nb][0:1, 0:T],
                               kvsb[b:b + 64, s, h, cs, HD:HD + 1], qcs4[b:b + 64, oc % 3, cs, :],
                               start=(cs == 0), stop=(cs == 1),
                               tile_position=tp, skip_group_check=True),
                           waits=(bank_rd[nb] if cs == 0 else []),
                           inc=(("sPE", 1) if cs == 1 else None))
                bank_rd[nb] = []
                cn = p.after("sPE")
                p.step("vector",
                       lambda e, nb=nb, oc=oc, hh=hh: e.tensor_scalar(
                           nrow4[0:1, oc % 2, hh, :], pb[nb][0:1, 0:T], EPS_ATT, None, ALU.add),
                       waits=[cn] + b1s["nrow_free"][oc % 2], inc=("sVE", 1))
                def _recip(e, oc=oc, hh=hh):
                    with nc.allow_low_precision(reason="attn norm reciprocal row stored bf16"):
                        return e.reciprocal(nrow4[0:1, oc % 2, hh, :], nrow4[0:1, oc % 2, hh, :])
                p.step("vector", _recip, waits=[p.after("sVE")], inc=("sVE", 1))
                nrow_done[(gh, oc, hh)] = p.after("sVE")
                bank_rd[nb] = [nrow_done[(gh, oc, hh)]]
                p.step("scalar",
                       lambda e, cb=cb, b=b, oc=oc: e.copy(
                           ctxsb[b:b + 64, oc % 2, :], pb[cb][b:b + 64, 0:T]),
                       waits=[cn] + b1s["ctxsb_free"][oc % 2], inc=("sACT", 1))
                ctx_evac[(gh, oc, hh)] = p.after("sACT")
                bank_rd[cb] = [ctx_evac[(gh, oc, hh)]]
            b1s["ctxsb_free"][oc % 2] = []
            b1s["nrow_free"][oc % 2] = []
            b1s["qcs_free"][oc % 3] = [p.after("sPE")]

        def emit_ctxb(gh, oc, bb0=6, bb1=7):
            for hh in range(2):
                b = hh * 64
                bb = (bb0, bb1)[hh]
                tp_bc = (0, b) if b else None
                p.step("tensor",
                       lambda e, bb=bb, b=b, oc=oc, hh=hh, tp=tp_bc: e.matmul(
                           pb[bb][b:b + 64, 0:T], ones_b[0:1, 0:64], nrow4[0:1, oc % 2, hh, :],
                           start=True, stop=True, tile_position=tp, skip_group_check=True),
                       waits=[nrow_done[(gh, oc, hh)], const_ve] + bank_rd[bb], inc=("sPE", 1))
                bank_rd[bb] = []
                bcm = p.after("sPE")
                p.step("vector",
                       lambda e, bb=bb, b=b, oc=oc: e.tensor_mul(
                           apre[b:b + 64, oc, :], pb[bb][b:b + 64, 0:T], ctxsb[b:b + 64, oc % 2, :]),
                       waits=[bcm, ctx_evac[(gh, oc, hh)]], inc=("sVE", 1))
                bank_rd[bb] = [p.after("sVE")]
            b1s["ctxsb_free"][oc % 2] = [p.after("sVE")]
            b1s["nrow_free"][oc % 2] = [p.after("sPE")]
            apre_all[gh] = p.after("sVE")

        def emit_wout(gh, half, wb0, wb1):
            for j, hc in enumerate(range(half * 4, half * 4 + 4)):
                wb = (wb0, wb1)[j % 2]
                buf, bufw = wo_buf[gh][hc // 4]
                for ec in range(KC):
                    p.step("tensor",
                           lambda e, wb=wb, ec=ec, hc=hc, buf=buf: e.matmul(
                               pb[wb][:], qa_view(buf)[:, ec, (hc % 4) * 128:(hc % 4) * 128 + 128],
                               apre[:, ec, :],
                               start=(ec == 0), stop=False, skip_group_check=True),
                           waits=([apre_all[gh], bufw] + bank_rd[wb]) if ec == 0 else [])
                bank_rd[wb] = []
                p.step("tensor",
                       lambda e, wb=wb, hc=hc: e.matmul(
                           pb[wb][:], bo_sb[0:1, hc * 128:(hc + 1) * 128], ones_b[0:1, :],
                           start=False, stop=True, skip_group_check=True),
                       inc=("sPE", 1))
                omm = p.after("sPE")
                p.step("vector",
                       lambda e, wb=wb, hc=hc, gh=gh: e.tensor_add(
                           acc[:, hc, gh * T:(gh + 1) * T],
                           acc[:, hc, gh * T:(gh + 1) * T], pb[wb][:]),
                       waits=[omm], inc=("sVE", 1))
                bank_rd[wb] = [p.after("sVE")]
            wout_done[gh] = p.after("sVE")

        hook_slices = {}

        def at_hook(h, fn):
            hook_slices.setdefault(h, []).append(fn)
        hook_ctr = [0]

        def run_hook(F):
            h = hook_ctr[0]
            hook_ctr[0] += 1
            for fn in hook_slices.get(h, []):
                fn(F)

        for gh in range(2):
            base = gh * 18
            at_hook(base + 0, lambda F, gh=gh: (
                emit_proj(gh, 0, F[0]), emit_proj(gh, 1, F[1])))
            at_hook(base + 2, lambda F, gh=gh: (
                emit_ctxa(gh, 0, F[2], F[3]), emit_proj(gh, 2, F[0]),
                emit_ctxa(gh, 1, F[2], F[3]), emit_proj(gh, 3, F[1])))
            at_hook(base + 4, lambda F, gh=gh: (
                emit_ctxb(gh, 0), emit_ctxa(gh, 2, F[2], F[3]),
                emit_proj(gh, 4, F[0]), emit_ctxb(gh, 1),
                emit_ctxa(gh, 3, F[2], F[3]), emit_proj(gh, 5, F[1])))
            at_hook(base + 6, lambda F, gh=gh: (
                emit_ctxb(gh, 2), emit_ctxa(gh, 4, F[2], F[3]),
                emit_proj(gh, 6, F[0]), emit_ctxb(gh, 3),
                emit_ctxa(gh, 5, F[2], F[3]), emit_proj(gh, 7, F[1])))
            at_hook(base + 8, lambda F, gh=gh: (
                emit_ctxb(gh, 4), emit_ctxa(gh, 6, F[2], F[3]),
                emit_ctxb(gh, 5), emit_ctxa(gh, 7, F[0], F[1])))
            at_hook(base + 10, lambda F, gh=gh: (
                emit_ctxb(gh, 6), emit_ctxb(gh, 7, F[2], F[3])))
            at_hook(base + 12, lambda F, gh=gh: emit_wout(gh, 0, F[0], F[1]))
            at_hook(base + 14, lambda F, gh=gh: emit_wout(gh, 1, F[2], F[3]))
        at_hook(7, lambda F: b1_release_wq(0))
        at_hook(9, lambda F: b1_load_wo(0))
        at_hook(15, lambda F: b1_release_wo(0))
        at_hook(17, lambda F: b1_load_wq(1))
        at_hook(25, lambda F: b1_release_wq(1))
        at_hook(27, lambda F: b1_load_wo(1))
        at_hook(33, lambda F: b1_release_wo(1))
        if g + 1 < N_GROUP:
            s_nx, gh2_nx = divmod(g + 1, 2)
            t0_nx = gh2_nx * G
            at_hook(33, lambda F, s_nx=s_nx, t0_nx=t0_nx: (
                emit_x_dma(x_in[s_nx, t0_nx:t0_nx + 128, :], 0),
                emit_x_dma(x_in[s_nx, t0_nx + 128:t0_nx + 256, :], 1)))
        b1_load_wq(0)

        gsb_rd = {0: [], 1: []}
        a_ch = {}
        a_wr_gate = [p.after("sPE")]
        ffn_gh = {}
        wbc_ready = {}
        wbc_wr = {0: [], 1: []}

        wbc_wre = {}

        def emit_wbc_a(eidx):
            for half in range(2):
                for s4 in range(4):
                    sub = half * 4 + s4
                    p.step("tensor",
                           lambda e, sub=sub, s4=s4, eidx=eidx: e.matmul(
                               pb[6][0:1, s4 * 128:(s4 + 1) * 128],
                               w_tm[:, sub, eidx:eidx + 1], ident[:],
                               start=True, stop=True, skip_group_check=True),
                           waits=((([wtm_done] if half == 0 else []) + bank_rd[6]) if s4 == 0 else []),
                           inc=(("sPE", 1) if s4 == 3 else None))
                bank_rd[6] = []
                wr_done = p.after("sPE")
                p.step("scalar",
                       lambda e, half=half: e.copy(
                           wrow[0:1, 0, half * 512:(half + 1) * 512], pb[6][0:1, 0:512]),
                       waits=[wr_done], inc=("sACT", 1))
                bank_rd[6] = [p.after("sACT")]
            wbc_wre[eidx] = p.after("sACT")

        def emit_wbc_b(eidx):
            wsl = eidx % 2
            wre = wbc_wre[eidx]
            for gh in range(2):
                p.step("tensor",
                       lambda e, gh=gh: e.matmul(
                           pb[6][:, 0:T], ones_f[:],
                           wrow[0:1, 0, gh * T:(gh + 1) * T],
                           start=True, stop=True, skip_group_check=True),
                       waits=([wre] if gh == 0 else []) + bank_rd[6],
                       inc=("sPE", 1))
                bank_rd[6] = []
                wb = p.after("sPE")
                p.step("scalar",
                       lambda e, gh=gh, wsl=wsl: e.copy(
                           wbc[:, wsl, gh * T:(gh + 1) * T], pb[6][:, 0:T]),
                       waits=[wb] + wbc_wr[wsl], inc=("sACT", 1))
                bank_rd[6] = [p.after("sACT")]
            wbc_ready[eidx] = p.after("sACT")

        def emit_wbc(eidx):
            emit_wbc_a(eidx)
            emit_wbc_b(eidx)

        for ei in range(9):
            routed = ei > 0
            eidx = ei - 1
            gsrc = gw_d[eidx] if routed else sgT
            usrc = uw_d[eidx] if routed else suT
            dsrc = dw_d[eidx] if routed else sdT
            wslot = eidx % 2
            if routed and eidx not in wbc_ready:
                emit_wbc(eidx)
            wbc_done = wbc_ready.get(eidx)

            for ih in range(2):
                for iq in range(2):
                    qi = ih * 2 + iq
                    gb, gld = ring_load(q_of(gsrc, qi * 512))
                    ub, uld = ring_load(q_of(usrc, qi * 512))
                    first_g, first_u = [gld], [uld]
                    for icq in range(4):
                        ic_h = iq * 4 + icq
                        for gh in range(2):
                            un = (qi * 4 + icq) * 2 + gh
                            gbank = un % 2
                            ubank = 2 + un % 2
                            eslot = un % 2
                            for kc in range(KC):
                                p.step("tensor",
                                       lambda e, gbank=gbank, kc=kc, gb=gb, icq=icq, gh=gh: e.matmul(
                                           pb[gbank][:], qa_view(gb)[:, kc, icq * 128:(icq + 1) * 128],
                                           xnf[:, kc, gh * T:(gh + 1) * T],
                                           start=(kc == 0), stop=(kc == KC - 1), skip_group_check=True),
                                       waits=(first_g + bank_rd[gbank]) if kc == 0 else [],
                                       inc=(("sPE", 1) if kc == KC - 1 else None))
                            first_g = []
                            bank_rd[gbank] = []
                            gmm = p.after("sPE")
                            for kc in range(KC):
                                p.step("tensor",
                                       lambda e, ubank=ubank, kc=kc, ub=ub, icq=icq, gh=gh: e.matmul(
                                           pb[ubank][:], qa_view(ub)[:, kc, icq * 128:(icq + 1) * 128],
                                           xnf[:, kc, gh * T:(gh + 1) * T],
                                           start=(kc == 0), stop=(kc == KC - 1), skip_group_check=True),
                                       waits=(first_u + bank_rd[ubank]) if kc == 0 else [],
                                       inc=(("sPE", 1) if kc == KC - 1 else None))
                            first_u = []
                            bank_rd[ubank] = []
                            umm = p.after("sPE")
                            p.step("scalar",
                                   lambda e, gbank=gbank, eslot=eslot: e.activation(
                                       gsb[:, eslot, :], pb[gbank][:], AF.Silu),
                                   waits=[gmm] + gsb_rd[eslot], inc=("sACT", 1))
                            gsb_rd[eslot] = []
                            bank_rd[gbank].append(p.after("sACT"))
                            if routed:
                                p.step("vector",
                                       lambda e, ubank=ubank, eslot=eslot, gh=gh, wslot=wslot: e.tensor_mul(
                                           usb[:, eslot, :], pb[ubank][:],
                                           wbc[:, wslot, gh * T:(gh + 1) * T]),
                                       waits=[umm, wbc_done], inc=("sVE", 1))
                                if routed and un == 31:
                                    wbc_wr[wslot] = [p.after("sVE")]
                            else:
                                p.step("vector",
                                       lambda e, ubank=ubank, eslot=eslot: e.tensor_copy(
                                           usb[:, eslot, :], pb[ubank][:]),
                                       waits=[umm], inc=("sVE", 1))
                            bank_rd[ubank].append(p.after("sVE"))
                            p.step("vector",
                                   lambda e, eslot=eslot, ic_h=ic_h, gh=gh: e.tensor_mul(
                                       a_e[:, ic_h, gh * T:(gh + 1) * T],
                                       gsb[:, eslot, :], usb[:, eslot, :]),
                                   waits=[p.after("sACT")] + a_wr_gate, inc=("sVE", 1))
                            a_wr_gate = []
                            a_ch[(ic_h, gh)] = p.after("sVE")
                            gsb_rd[eslot].append(p.after("sVE"))
                    ring_done(gb)
                    ring_done(ub)
                run_hook([0, 1, 2, 3])
                if ih == 1 and ei + 1 <= 8 and ei in wbc_wre and ei not in wbc_ready:
                    emit_wbc_b(ei)
                dq = []
                for q2 in range(2):
                    r0 = ih * 1024 + q2 * 512
                    dq.append(ring_load(
                        dsrc[r0:r0 + 512, :].rearrange("(c k) h -> k c h", k=128), kdim=4))
                first_d = [dq[0][1], dq[1][1], p.after("sVE")]
                for gh in range(2):
                    for hc in range(KC):
                        dn = gh * KC + hc
                        dbank = 4 + dn % 2
                        for icq in range(KC):
                            buf, bufw = dq[icq // 4]
                            w = []
                            if icq == 0:
                                w = first_d + bank_rd[dbank]
                            p.step("tensor",
                                   lambda e, dbank=dbank, icq=icq, hc=hc, gh=gh, buf=buf: e.matmul(
                                       pb[dbank][:], qd_view(buf)[:, icq % 4, hc * 128:(hc + 1) * 128],
                                       a_e[:, icq, gh * T:(gh + 1) * T],
                                       start=(icq == 0), stop=(icq == KC - 1), skip_group_check=True),
                                   waits=w,
                                   inc=(("sPE", 1) if icq == KC - 1 else None))
                        first_d = []
                        bank_rd[dbank] = []
                        dmm = p.after("sPE")
                        p.step("vector",
                               lambda e, dbank=dbank, hc=hc, gh=gh: e.tensor_add(
                                   acc[:, hc, gh * T:(gh + 1) * T],
                                   acc[:, hc, gh * T:(gh + 1) * T], pb[dbank][:]),
                               waits=[dmm] + ([acc_init_done] if ei == 0 and ih == 0 else []),
                               inc=("sVE", 1))
                        bank_rd[dbank].append(p.after("sVE"))
                    if ei == 8 and ih == 1:
                        ffn_gh[gh] = p.after("sVE")
                for buf, _ in dq:
                    ring_done(buf)
                a_wr_gate = [p.after("sPE")]
                run_hook([4, 5, 2, 3])
                if ih == 0 and ei + 1 <= 8 and ei not in wbc_wre:
                    emit_wbc_a(ei)
        assert hook_ctr[0] == 36 and max(hook_slices) < 36
        ffn_done = p.after("sVE")
        if g == 0:
            tap("acc_g0", acc[:])

        b5_gate = p.after("sACT")
        ostate = {0: [], 1: []}
        for sub in range(N_SUB):
            l0 = t0 + sub * 128
            for rnd in range(4):
                bk = 6 + rnd % 2
                ev = bank_rd[bk]
                for j in range(2):
                    hc = rnd * 2 + j
                    w = []
                    if j == 0:
                        w = [ffn_gh[sub // 4], b5_gate] + ev
                    p.step("tensor",
                           lambda e, hc=hc, j=j, sub=sub, bk=bk: e.transpose(
                               pb[bk][:, j * 128:(j + 1) * 128],
                               acc[:, hc, sub * 128:sub * 128 + 128], ident[:]),
                           waits=w, inc=(("sPE", 1) if j == 1 else None))
                bank_rd[bk] = []
                tp = p.after("sPE")
                p.step("scalar",
                       lambda e, rnd=rnd, bk=bk, osl=sub % 2: e.copy(
                           outst[:, osl, rnd * 256:(rnd + 1) * 256], pb[bk][:, 0:256]),
                       waits=[tp] + ostate[sub % 2], inc=("sACT", 1))
                bank_rd[bk] = [p.after("sACT")]
            ostate[sub % 2] = []
            p.step("scalar",
                   lambda e, s=s, l0=l0, osl=sub % 2: e.dma_start(
                       out_d[s, l0:l0 + 128, :], outst[:, osl]),
                   inc=(f"sO{sub % 2}", 16))
            ostate[sub % 2] = [p.after(f"sO{sub % 2}")]
            out_cnt[0] += 16

    p.step("scalar", None, waits=[p.after("sO0"), p.after("sO1")])

    with nc.Block() as blk:
        for engname in ("sync", "gpsimd", "vector", "scalar", "tensor"):
            def body(eng, engname=engname):
                for (e, fn, waits, inc) in p.steps:
                    if e != engname:
                        continue
                    for (sname, v) in waits:
                        eng.wait_ge(sems[sname], v)
                    if fn is not None:
                        ins = fn(eng)
                        if inc is not None:
                            ins.then_inc(sems[inc[0]], inc[1])
            getattr(blk, engname)(body)

    return nc


_NC_CACHE = [None]


def kernel(L0_attn_x, L0_attn_W_qkv, L0_attn_b_qkv, L0_attn_W_out, L0_attn_b_out,
           L0_ffn_shared_gate, L0_ffn_shared_up, L0_ffn_shared_down,
           L0_ffn_gate_w, L0_ffn_up_w, L0_ffn_down_w,
           L0_ffn_router_w, L0_ffn_routing_bias):
    f32 = lambda a: np.ascontiguousarray(np.asarray(a, dtype=np.float32))
    x = f32(L0_attn_x)

    pos = np.arange(L, dtype=np.float64) / L * (np.pi / 2)
    cosr = np.cos(pos).astype(np.float32)
    sinr = np.sin(pos).astype(np.float32)

    common = {
        "wqkv": f32(L0_attn_W_qkv),
        "bqkv": f32(L0_attn_b_qkv).reshape(1, 3 * D),
        "wo": f32(L0_attn_W_out),
        "bo": f32(L0_attn_b_out).reshape(1, D),
        "sgT": np.ascontiguousarray(f32(L0_ffn_shared_gate).T),
        "suT": np.ascontiguousarray(f32(L0_ffn_shared_up).T),
        "sdT": np.ascontiguousarray(f32(L0_ffn_shared_down).T),
        "gw": f32(L0_ffn_gate_w),
        "uw": f32(L0_ffn_up_w),
        "dw": f32(L0_ffn_down_w),
        "rwT": np.ascontiguousarray(f32(L0_ffn_router_w).T),
        "rbias": f32(L0_ffn_routing_bias).reshape(1, NE),
        "identc": np.eye(128, dtype=np.float32),
        "costab": np.ascontiguousarray(np.broadcast_to(cosr, (128, L))),
        "sintab": np.ascontiguousarray(np.broadcast_to(sinr, (128, L))),
        "coscol": np.ascontiguousarray(cosr.reshape(16, 128).T),
        "sincol": np.ascontiguousarray(sinr.reshape(16, 128).T),
        "idx8": np.ascontiguousarray(np.broadcast_to(np.arange(NE, dtype=np.float32), (128, NE))),
        "idxb8": np.ascontiguousarray(np.broadcast_to(np.arange(NE, dtype=np.float32) + BIG, (128, NE))),
    }

    if _NC_CACHE[0] is None:
        _NC_CACHE[0] = build_program()
    nc = _NC_CACHE[0]

    in_maps = []
    for c in range(N_CORES):
        m = dict(common)
        m["x"] = np.ascontiguousarray(x[c * B_CORE:(c + 1) * B_CORE])
        in_maps.append(m)

    res = run_bass_kernel_spmd(nc, in_maps, core_ids=list(range(N_CORES)))
    return np.concatenate([r["out"] for r in res.results], axis=0)


# version 20 (final)
# speedup vs baseline: 1.1925x; 1.1925x over previous
import numpy as np
import ml_dtypes

import concourse.bass as bass
import concourse.mybir as mybir
from concourse.bass_utils import run_bass_kernel_spmd

F32 = mybir.dt.float32
BF16 = mybir.dt.bfloat16
AF = mybir.ActivationFunctionType
ALU = mybir.AluOpType
AX = mybir.AxisListType

N_CORES = 8
B_CORE = 4
L = 2048
D = 1024
NH = 16
HD = 64
I_FF = 2048
NE = 8
EPS_LN = 1e-5
EPS_ATT = 1e-4
BIG = 65536.0

N_GROUP = 8
G = 1024
T = 512
N_SUB = 8
KC = 8
NRING = 6


class Prog:
    def __init__(self):
        self.steps = []
        self.cnt = {}

    def n(self, sem):
        return self.cnt.get(sem, 0)

    def after(self, sem):
        return (sem, self.n(sem))

    def step(self, eng, fn=None, waits=(), inc=None):
        ws = []
        for (s, v) in waits:
            assert v <= self.n(s), f"wait on future count {s}={v} > {self.n(s)}"
            if v > 0:
                ws.append((s, v))
        if inc is not None:
            assert fn is not None
            self.cnt[inc[0]] = self.cnt.get(inc[0], 0) + inc[1]
        self.steps.append((eng, fn, tuple(ws), inc))


_TAPS = {}


def build_program():
    nc = bass.Bass("TRN2", target_bir_lowering=False, debug=False)

    x_in = nc.dram_tensor("x", [B_CORE, L, D], F32, kind="ExternalInput").ap()
    wqkv = nc.dram_tensor("wqkv", [D, 3 * D], F32, kind="ExternalInput").ap()
    bqkv = nc.dram_tensor("bqkv", [1, 3 * D], F32, kind="ExternalInput").ap()
    wo_d = nc.dram_tensor("wo", [D, D], F32, kind="ExternalInput").ap()
    bo_d = nc.dram_tensor("bo", [1, D], F32, kind="ExternalInput").ap()
    sgT = nc.dram_tensor("sgT", [D, I_FF], F32, kind="ExternalInput").ap()
    suT = nc.dram_tensor("suT", [D, I_FF], F32, kind="ExternalInput").ap()
    sdT = nc.dram_tensor("sdT", [I_FF, D], F32, kind="ExternalInput").ap()
    gw_d = nc.dram_tensor("gw", [NE, D, I_FF], F32, kind="ExternalInput").ap()
    uw_d = nc.dram_tensor("uw", [NE, D, I_FF], F32, kind="ExternalInput").ap()
    dw_d = nc.dram_tensor("dw", [NE, I_FF, D], F32, kind="ExternalInput").ap()
    rwT_d = nc.dram_tensor("rwT", [D, NE], F32, kind="ExternalInput").ap()
    rb_d = nc.dram_tensor("rbias", [1, NE], F32, kind="ExternalInput").ap()
    id_d = nc.dram_tensor("identc", [128, 128], F32, kind="ExternalInput").ap()
    ctab_d = nc.dram_tensor("costab", [128, L], F32, kind="ExternalInput").ap()
    stab_d = nc.dram_tensor("sintab", [128, L], F32, kind="ExternalInput").ap()
    ccol_d = nc.dram_tensor("coscol", [128, 16], F32, kind="ExternalInput").ap()
    scol_d = nc.dram_tensor("sincol", [128, 16], F32, kind="ExternalInput").ap()
    idx_d = nc.dram_tensor("idx8", [128, NE], F32, kind="ExternalInput").ap()
    idxb_d = nc.dram_tensor("idxb8", [128, NE], F32, kind="ExternalInput").ap()
    out_d = nc.dram_tensor("out", [B_CORE, L, D], F32, kind="ExternalOutput").ap()
    tap_d = {}
    for tname, (shape, dt) in _TAPS.items():
        tap_d[tname] = nc.dram_tensor("tap_" + tname, shape, dt, kind="ExternalOutput").ap()

    A = nc.alloc_sbuf_tensor
    xnf = A("xnf", [128, KC, G], BF16)
    _acc_off = nc._sbuf_addr_for_side(None)
    acc = A("acc", [128, KC, G], F32)
    kcs2 = nc.alloc_sbuf_tensor_at("kcs2s", [128, 2, NH, 2, HD], BF16, offset=_acc_off)
    v2 = nc.alloc_sbuf_tensor_at("v2s", [128, 2, D], BF16, offset=_acc_off + 8192)
    _ae_off = nc._sbuf_addr_for_side(None)
    a_e = A("a_e", [128, KC, G], BF16)
    xnfa = nc.alloc_sbuf_tensor_at("xnfas", [128, 2, KC, 128], BF16, offset=_ae_off)
    kvst = nc.alloc_sbuf_tensor_at("kvst", [128, NH, HD + 1], BF16, offset=_ae_off + 10240)
    wring = A("wring", [128, NRING, 4096], BF16)
    costab = A("costabs", [128, G], BF16)
    sintab = A("sintabs", [128, G], BF16)
    coscol = A("coscols", [128, 16], F32)
    sincol = A("sincols", [128, 16], F32)
    ident = A("idents", [128, 128], F32)
    kvsb = A("kvsbs", [128, B_CORE, NH, 2, HD + 1], BF16)
    wbc = A("wbcs", [128, 2, G], F32)
    xstage = A("xstage", [128, 2, D], F32)
    outst = A("outsts", [128, 2, D], F32)
    gsb = A("gsbs", [128, 2, T], BF16)
    usb = A("usbs", [128, 2, T], BF16)
    rw_sb = A("rw_sbs", [128, KC, NE], F32)
    rb_sb = A("rb_sbs", [1, NE], F32)
    idx8 = A("idx8s", [128, NE], F32)
    idxb8 = A("idxb8s", [128, NE], F32)
    probs = A("probss", [128, N_SUB, NE], F32)
    rtmp = A("rtmps", [128, N_SUB, NE], F32)
    rtmp2 = A("rtmp2s", [128, N_SUB, NE], F32)
    rm1 = A("rm1s", [128, N_SUB], F32)
    rm2 = A("rm2s", [128, N_SUB], F32)
    ri_ = A("ris", [128, N_SUB], F32)
    rmask1 = A("rmask1s", [128, N_SUB, NE], F32)
    rmask2 = A("rmask2s", [128, N_SUB, NE], F32)
    rden = A("rdens", [128, N_SUB], F32)
    w_tm = A("w_tms", [128, N_SUB, NE], F32)
    wrow = A("wrows", [1, 1, G], F32)
    ones_b = A("ones_b", [1, T], BF16)
    zrow = A("zrows", [1, T], BF16)
    ones_f = A("ones_f", [1, 128], F32)
    onesc = A("onescs", [128, 1], BF16)
    epsln = A("epslns", [128, 1], F32)
    bq_sb = A("bq_sbs", [1, D], BF16)
    bo_sb = A("bo_sbs", [1, D], BF16)
    bnst = A("bnsts", [128, 12], F32)
    bnmv = A("bnmvs", [128, 2], F32)
    lnsq = A("lnsqs", [128, 1], F32)
    lnr = A("lnrs", [128, 1], F32)
    xn_tm = A("xn_tms", [128, 2, D], F32)
    xnf32 = A("xnf32s", [128, D], F32)
    qrel = A("qrels", [128, 2, T], BF16)
    qcs4 = A("qcs4s", [128, 3, 2, T], BF16)
    ctxsb = A("ctxsbs", [128, 2, T], BF16)
    nrow4 = A("nrow4s", [1, 2, 2, T], BF16)
    apre = A("apres", [128, KC, T], BF16)

    pb = [nc.alloc_psum_tensor(f"pb{i}", [128, 512], F32) for i in range(8)]

    sems = {s: nc.alloc_semaphore(s) for s in
            ("sWC", "sXC", "sKV", "sTAB", "sXA", "sXR", "sO0", "sO1",
             "sPE", "sVE", "sACT")}
    for i in range(NRING):
        sems[f"sR{i}"] = nc.alloc_semaphore(f"sR{i}")

    p = Prog()
    out_cnt = [0]
    b1_next = [None]

    def dma(q, dst, src, sem, waits=()):
        p.step(q, lambda e, d=dst, s=src: e.dma_start(d, s), waits=waits, inc=(sem, 16))

    def tap(tname, src_ap):
        if tname in tap_d:
            p.step("sync", lambda e, d=tap_d[tname], s=src_ap: e.dma_start(d, s),
                   waits=[p.after("sPE"), p.after("sVE"), p.after("sACT"), p.after("sKV")],
                   inc=("sO0", 16))
            out_cnt[0] += 16

    dma("gpsimd", idx8[:], idx_d, "sWC")
    dma("gpsimd", idxb8[:], idxb_d, "sWC")
    dma("gpsimd", bq_sb[:], bqkv[0:1, 0:D], "sWC")
    dma("gpsimd", wring[0:1, 4, 0:2048], bqkv[0:1, D:3 * D], "sWC")
    dma("gpsimd", bo_sb[:], bo_d, "sWC")
    dma("gpsimd", coscol[:], ccol_d, "sWC")
    dma("gpsimd", sincol[:], scol_d, "sWC")
    w_const = p.after("sWC")
    dma("sync", ident[:], id_d, "sXC")
    dma("sync", rb_sb[:], rb_d, "sXC")
    dma("sync", rw_sb[:], rwT_d.rearrange("(c k) e -> k c e", k=128), "sXC")
    x_const = p.after("sXC")
    p.step("vector", lambda e: e.memset(ones_b[:], 1.0), inc=("sVE", 1))
    p.step("vector", lambda e: e.memset(zrow[:], 0.0), inc=("sVE", 1))
    p.step("vector", lambda e: e.memset(ones_f[:], 1.0), inc=("sVE", 1))
    p.step("vector", lambda e: e.memset(onesc[:], 1.0), inc=("sVE", 1))
    p.step("vector", lambda e: e.memset(epsln[:], EPS_LN), inc=("sVE", 1))
    const_ve = p.after("sVE")

    ring_free = [None] * NRING
    ring_cnt = [0]
    ring_reserved = set()
    ring_perm = set()

    def ring_load(src_ap, kdim=KC, slot=None):
        if slot is None:
            while True:
                b = ring_cnt[0] % NRING
                ring_cnt[0] += 1
                if b not in ring_reserved:
                    break
        else:
            b = slot
            ring_reserved.add(b)
        waits = [ring_free[b]] if ring_free[b] is not None else []
        p.step("gpsimd",
               lambda e, b=b, s=src_ap, kdim=kdim: e.dma_start(
                   wring[:, b].rearrange("p (k i) -> p k i", k=kdim), s),
               waits=waits, inc=(f"sR{b}", 16))
        return b, p.after(f"sR{b}")

    def ring_done(b):
        ring_free[b] = p.after("sPE")
        if b not in ring_perm:
            ring_reserved.discard(b)

    def qa_view(b):
        return wring[:, b].rearrange("p (k i) -> p k i", k=KC)

    def qd_view(b):
        return wring[:, b].rearrange("p (k i) -> p k i", k=4)

    def q_of(w2d, c0):
        return w2d[:, c0:c0 + 512].rearrange("(c k) i -> k c i", k=128)

    ln_rd = {0: [], 1: []}
    ln_ld = {0: None, 1: None}

    def emit_x_dma(x_src, xslot):
        sem = "sXA" if xslot == 0 else "sXR"
        dma("sync", xstage[:, xslot], x_src, sem, waits=ln_rd[xslot])
        ln_rd[xslot] = []
        ln_ld[xslot] = p.after(sem)

    def emit_ln_vec(xslot):
        ld = ln_ld[xslot]
        p.step("vector", lambda e, xslot=xslot: e.bn_stats(bnst[:, 0:6], xstage[:, xslot, 0:512]),
               waits=[ld, const_ve])
        p.step("vector", lambda e, xslot=xslot: e.bn_stats(bnst[:, 6:12], xstage[:, xslot, 512:1024]),
               inc=("sVE", 1))
        p.step("vector", lambda e: e.bn_aggr(bnmv[:], bnst[:]),
               waits=[p.after("sVE"), p.after("sACT")], inc=("sVE", 1))
        p.step("scalar", lambda e: e.activation(lnsq[:], bnmv[:, 1:2], AF.Sqrt, bias=epsln[:]),
               waits=[p.after("sVE")], inc=("sACT", 1))
        p.step("vector", lambda e: e.reciprocal(lnr[:], lnsq[:]),
               waits=[p.after("sACT")], inc=("sVE", 1))
        p.step("vector",
               lambda e, xslot=xslot: e.tensor_scalar(
                   xn_tm[:, xslot], xstage[:, xslot], bnmv[:, 0:1], lnr[:], ALU.subtract, ALU.mult),
               waits=[p.after("sVE")] + xn_gate[xslot], inc=("sVE", 1))
        ln_rd[xslot] = [p.after("sVE")]
        ln_done[xslot] = p.after("sVE")
        return ln_done[xslot]

    bank_rd = {i: [] for i in range(8)}
    xn_gate = {0: [], 1: []}
    ln_done = {0: None, 1: None}

    def emit_transposes(xslot, dst_bf_slices, dst_f32=None, banks=(5, 7)):
        xn_ready = ln_done[xslot]
        for rnd in range(4):
            bk = banks[rnd % 2]
            ev = bank_rd[bk]
            for j in range(2):
                c = rnd * 2 + j
                w = []
                if j == 0:
                    w = [xn_ready, x_const] + ev
                    if rnd == 0:
                        w.append(p.after("sACT"))
                p.step("tensor",
                       lambda e, c=c, j=j, bk=bk, xslot=xslot: e.transpose(
                           pb[bk][:, j * 128:(j + 1) * 128],
                           xn_tm[:, xslot, c * 128:(c + 1) * 128], ident[:]),
                       waits=w, inc=(("sPE", 1) if j == 1 else None))
            bank_rd[bk] = []
            tp = p.after("sPE")
            p.step("scalar", lambda e, d=dst_bf_slices[rnd], bk=bk: e.copy(
                       d, pb[bk][:, 0:256].rearrange("p (c t) -> p c t", t=128)),
                   waits=[tp], inc=("sACT", 1))
            if dst_f32 is not None:
                p.step("scalar",
                       lambda e, rnd=rnd, bk=bk: e.copy(
                           dst_f32[:, rnd * 256:(rnd + 1) * 256], pb[bk][:, 0:256]),
                       inc=("sACT", 1))
            bank_rd[bk] = [p.after("sACT")]
        xn_gate[xslot] = [p.after("sPE")]
        return p.after("sACT")

    kvq = [ring_load(q_of(wqkv, D + qi * 512)) for qi in range(4)]

    tpa_done = {0: [], 1: []}
    kcs_done2 = {}
    v_done2 = {}
    kcs_free = {0: [], 1: []}
    qrel_rd = {0: [], 1: []}

    def tp_round_a(slot, rnd):
        ev = bank_rd[7]
        for j in range(2):
            c = rnd * 2 + j
            w = ([ln_done[slot], x_const] + ev) if j == 0 else []
            p.step("tensor",
                   lambda e, c=c, j=j, slot=slot: e.transpose(
                       pb[7][:, j * 128:(j + 1) * 128],
                       xn_tm[:, slot, c * 128:(c + 1) * 128], ident[:]),
                   waits=w, inc=(("sPE", 1) if j == 1 else None))
        bank_rd[7] = []
        tp = p.after("sPE")
        p.step("scalar",
               lambda e, rnd=rnd, slot=slot: e.copy(
                   xnfa[:, slot, rnd * 2:(rnd + 1) * 2, :],
                   pb[7][:, 0:256].rearrange("p (c t) -> p c t", t=128)),
               waits=[tp], inc=("sACT", 1))
        bank_rd[7] = [p.after("sACT")]
        if rnd == 3:
            tpa_done[slot] = [p.after("sACT")]
            xn_gate[slot] = [p.after("sPE")]

    def emit_kv_a(sub2):
        slot2 = sub2 % 2
        for half in range(2):
            for hh in range(8):
                h = half * 8 + hh
                col = hh * 64
                p.step("tensor",
                       lambda e, h=h, bank=half, col=col, sub2=sub2, slot2=slot2: e.matmul(
                           pb[bank][:, col:col + 64],
                           kcs2[:, slot2, h].rearrange("p a d -> p (a d)"),
                           v2[:, slot2, h * 64:(h + 1) * 64],
                           start=False, stop=(sub2 == 15),
                           skip_group_check=True),
                       waits=([kcs_done2[slot2], v_done2[slot2]] if (half == 0 and hh == 0) else []))
                p.step("tensor",
                       lambda e, h=h, sub2=sub2, slot2=slot2: e.matmul(
                           pb[6][:, h:h + 1],
                           kcs2[:, slot2, h].rearrange("p a d -> p (a d)"), onesc[:],
                           start=False, stop=(sub2 == 15),
                           skip_group_check=True),
                       inc=(("sPE", 1) if (half == 1 and hh == 7) else None))
        kcs_free[slot2] = [p.after("sPE")]

    for s in range(B_CORE):
        for bank in (0, 1, 6):
            p.step("tensor",
                   lambda e, bank=bank: e.matmul(
                       pb[bank][:], ones_b[0:1, 0:128], zrow[:],
                       start=True, stop=True, skip_group_check=True),
                   waits=[const_ve] if s == 0 else [],
                   inc=(("sPE", 1) if bank == 6 else None))
        emit_x_dma(x_in[s, 0:128, :], 0)
        emit_x_dma(x_in[s, 128:256, :], 1)
        emit_ln_vec(0)
        emit_x_dma(x_in[s, 256:384, :], 0)
        emit_ln_vec(1)
        for rnd in range(4):
            tp_round_a(0, rnd)
        for sub in range(16):
            slot = sub % 2
            if sub + 2 < 16:
                emit_ln_vec(slot)
            grp_done = {}
            for grp in range(4):
                bank = 2 + grp
                buf, bufw = kvq[grp]
                for kc in range(KC):
                    p.step("tensor",
                           lambda e, bank=bank, kc=kc, buf=buf, slot=slot: e.matmul(
                               pb[bank][:], xnfa[:, slot, kc, :], qa_view(buf)[:, kc, :],
                               start=(kc == 0), stop=False, skip_group_check=True),
                           waits=(tpa_done[slot] + [bufw] + bank_rd[bank]) if kc == 0 else [])
                bank_rd[bank] = []
                c0 = (grp // 2) * 1024 + (grp % 2) * 512
                p.step("tensor",
                       lambda e, bank=bank, c0=c0: e.matmul(
                           pb[bank][:], ones_b[0:1, 0:128], wring[0:1, 4, c0:c0 + 512],
                           start=False, stop=True, skip_group_check=True),
                       waits=([w_const, const_ve] if (s == 0 and sub == 0) else []),
                       inc=("sPE", 1))
                grp_done[grp] = p.after("sPE")
                if sub + 1 < 16:
                    tp_round_a((sub + 1) % 2, grp)
            for half in range(2):
                p.step("scalar",
                       lambda e, half=half: e.activation(qrel[:, half], pb[2 + half][:], AF.Relu),
                       waits=[grp_done[half]] + qrel_rd[half], inc=("sACT", 1))
                qrel_rd[half] = []
                kr = p.after("sACT")
                bank_rd[2 + half] = [kr]
                p.step("scalar",
                       lambda e, half=half, slot=slot: e.copy(
                           v2[:, slot, half * 512:(half + 1) * 512], pb[4 + half][:]),
                       waits=[grp_done[2 + half]] + kcs_free[slot], inc=("sACT", 1))
                bank_rd[4 + half] = [p.after("sACT")]
                if half == 1:
                    v_done2[slot] = p.after("sACT")
                p.step("vector",
                       lambda e, half=half, sub=sub, slot=slot: e.tensor_scalar(
                           kcs2[:, slot, half * 8:(half + 1) * 8, 0, :],
                           qrel[:, half].rearrange("p (h d) -> p h d", d=HD),
                           coscol[:, sub:sub + 1], None, ALU.mult),
                       waits=[kr, w_const] + kcs_free[slot], inc=("sVE", 1))
                p.step("vector",
                       lambda e, half=half, sub=sub, slot=slot: e.tensor_scalar(
                           kcs2[:, slot, half * 8:(half + 1) * 8, 1, :],
                           qrel[:, half].rearrange("p (h d) -> p h d", d=HD),
                           sincol[:, sub:sub + 1], None, ALU.mult),
                       inc=("sVE", 1))
                qrel_rd[half] = [p.after("sVE")]
            kcs_free[slot] = []
            kcs_done2[slot] = p.after("sVE")
            if sub + 3 < 16:
                emit_x_dma(x_in[s, (sub + 3) * 128:(sub + 4) * 128, :], (sub + 3) % 2)
            if sub >= 1:
                emit_kv_a(sub - 1)
        emit_kv_a(15)
        kv_done = p.after("sPE")
        p.step("scalar",
               lambda e: e.copy(kvst[:, 0:8, 0:HD],
                                pb[0][:].rearrange("p (h d) -> p h d", d=HD)),
               waits=[kv_done, p.after("sKV")], inc=("sACT", 1))
        p.step("scalar",
               lambda e: e.copy(kvst[:, 8:16, 0:HD],
                                pb[1][:].rearrange("p (h d) -> p h d", d=HD)),
               inc=("sACT", 1))
        p.step("scalar", lambda e: e.copy(kvst[:, :, HD], pb[6][:, 0:16]),
               inc=("sACT", 1))
        kv_evac = p.after("sACT")
        kvst_v = kvst.rearrange("p (h e2) d -> p h e2 d", e2=2)
        kvsb_v = kvsb.rearrange("p s (h e2) c d -> p s h e2 c d", e2=2)
        for par in range(2):
            for cs in range(2):
                dma("gpsimd",
                    kvsb_v[par * 64:(par + 1) * 64, s, :, par, cs, :],
                    kvst_v[cs * 64:(cs + 1) * 64, :, par, :],
                    "sKV", waits=[kv_evac])
        kv_moved = p.after("sKV")
        p.step("tensor", None, waits=[kv_evac])
    kv_ready = p.after("sKV")
    for qi in range(4):
        ring_done(kvq[qi][0])
    ring_free[4] = p.after("sPE")
    ring_perm.update({4, 5})
    ring_reserved.update({4, 5})
    for i in range(8):
        bank_rd[i] = [p.after("sACT"), p.after("sVE")]

    tap("kvsb", kvsb[:])
    b1s = {
        "qcs_free": {i: [] for i in range(3)},
        "qrel_free": {i: [] for i in range(2)},
        "ctxsb_free": {i: [] for i in range(2)},
        "nrow_free": {i: [] for i in range(2)},
    }
    for g in range(N_GROUP):
        s, gh2 = divmod(g, 2)
        t0 = gh2 * G

        dma("gpsimd", costab[:], ctab_d[:, t0:t0 + G], "sTAB", waits=[p.after("sVE")])
        dma("gpsimd", sintab[:], stab_d[:, t0:t0 + G], "sTAB")
        tab_ld = p.after("sTAB")

        def emit_x_tp(sub, slot):
            for rnd in range(4):
                bk = (5, 7)[rnd % 2]
                ev = bank_rd[bk]
                for j in range(2):
                    w = ([ln_ld[slot], x_const] + ev) if j == 0 else []
                    p.step("tensor",
                           lambda e, c=rnd * 2 + j, j=j, bk=bk, slot=slot: e.transpose(
                               pb[bk][:, j * 128:(j + 1) * 128],
                               xstage[:, slot, c * 128:(c + 1) * 128], ident[:]),
                           waits=w, inc=(("sPE", 1) if j == 1 else None))
                bank_rd[bk] = []
                tp = p.after("sPE")
                p.step("scalar",
                       lambda e, rnd=rnd, bk=bk, sub=sub: e.copy(
                           acc[:, rnd * 2:(rnd + 1) * 2, sub * 128:sub * 128 + 128],
                           pb[bk][:, 0:256].rearrange("p (c t) -> p c t", t=128)),
                       waits=[tp], inc=("sACT", 1))
                bank_rd[bk] = [p.after("sACT")]
            ln_rd[slot].append(p.after("sPE"))

        if g == 0:
            emit_x_dma(x_in[s, t0:t0 + 128, :], 0)
            emit_x_dma(x_in[s, t0 + 128:t0 + 256, :], 1)
        emit_ln_vec(0)
        for sub in range(N_SUB):
            l0 = t0 + sub * 128
            if sub + 1 < N_SUB:
                emit_ln_vec((sub + 1) % 2)
            emit_transposes(
                sub % 2,
                [xnf[:, r * 2:(r + 1) * 2, sub * 128:sub * 128 + 128] for r in range(4)],
                dst_f32=xnf32)
            xf_done = p.after("sACT")
            emit_x_tp(sub, sub % 2)
            if sub + 2 < N_SUB:
                emit_x_dma(x_in[s, l0 + 256:l0 + 384, :], sub % 2)
            for kc in range(KC):
                p.step("tensor",
                       lambda e, kc=kc, sub=sub: e.matmul(
                           pb[6][:, sub * 8:(sub + 1) * 8],
                           xnf32[:, kc * 128:(kc + 1) * 128], rw_sb[:, kc, :],
                           start=(kc == 0), stop=False, skip_group_check=True),
                       waits=([xf_done, x_const] + bank_rd[6] if kc == 0 else []))
            if sub == 0:
                bank_rd[6] = []
            p.step("tensor",
                   lambda e, sub=sub: e.matmul(
                       pb[6][:, sub * 8:(sub + 1) * 8], ones_f[:], rb_sb[:],
                       start=False, stop=True, skip_group_check=True),
                   waits=[const_ve], inc=("sPE", 1))
        acc_init_done = p.after("sACT")
        rt_done = p.after("sPE")
        p.step("scalar",
               lambda e: e.activation(probs.rearrange("p s e -> p (s e)"), pb[6][:, 0:64], AF.Sigmoid),
               waits=[rt_done, p.after("sVE")], inc=("sACT", 1))
        probs_done = p.after("sACT")
        bank_rd[6] = [probs_done]
        if g == 0:
            tap("xnf_g0", xnf[:])
            tap("probs_g0", probs[:])

        def rstep(fn, waits=()):
            for i in range(N_SUB):
                p.step("vector", (lambda e, i=i: fn(e, i)),
                       waits=(waits if i == 0 else ()),
                       inc=(("sVE", 1) if i == N_SUB - 1 else None))
        rstep(lambda e, i: e.reduce_max(rm1[:, i:i + 1], probs[:, i], axis=AX.X),
              waits=[probs_done, w_const])
        rstep(lambda e, i: e.tensor_scalar(rtmp[:, i], probs[:, i], rm1[:, i:i + 1], None, ALU.is_equal))
        rstep(lambda e, i: e.scalar_tensor_tensor(rtmp[:, i], rtmp[:, i], -BIG, idxb8[:], ALU.mult, ALU.add))
        rstep(lambda e, i: e.tensor_reduce(ri_[:, i:i + 1], rtmp[:, i], axis=AX.X, op=ALU.min))
        rstep(lambda e, i: e.tensor_scalar(rmask1[:, i], idx8[:], ri_[:, i:i + 1], None, ALU.is_equal))
        rstep(lambda e, i: e.scalar_tensor_tensor(rtmp2[:, i], rmask1[:, i], -BIG, probs[:, i], ALU.mult, ALU.add))
        rstep(lambda e, i: e.reduce_max(rm2[:, i:i + 1], rtmp2[:, i], axis=AX.X))
        rstep(lambda e, i: e.tensor_scalar(rtmp[:, i], rtmp2[:, i], rm2[:, i:i + 1], None, ALU.is_equal))
        rstep(lambda e, i: e.scalar_tensor_tensor(rtmp[:, i], rtmp[:, i], -BIG, idxb8[:], ALU.mult, ALU.add))
        rstep(lambda e, i: e.tensor_reduce(ri_[:, i:i + 1], rtmp[:, i], axis=AX.X, op=ALU.min))
        rstep(lambda e, i: e.tensor_scalar(rmask2[:, i], idx8[:], ri_[:, i:i + 1], None, ALU.is_equal))
        rstep(lambda e, i: e.tensor_add(rtmp[:, i], rmask1[:, i], rmask2[:, i]))
        rstep(lambda e, i: e.tensor_add(rden[:, i:i + 1], rm1[:, i:i + 1], rm2[:, i:i + 1]))
        rstep(lambda e, i: e.reciprocal(rm1[:, i:i + 1], rden[:, i:i + 1]))
        rstep(lambda e, i: e.scalar_tensor_tensor(w_tm[:, i], probs[:, i], rm1[:, i:i + 1], rtmp[:, i],
                                                  ALU.mult, ALU.mult))
        wtm_done = p.after("sVE")
        if g == 0:
            tap("wtm_g0", w_tm[:])

        qcs_done = {}
        nrow_done = {}
        ctx_evac = {}
        apre_all = {}
        wout_done = {}
        wq_buf = {}
        wo_buf = {}

        def b1_load_wq(gh):
            wq_buf[gh] = [ring_load(q_of(wqkv, qi * 512), slot=4 + qi) for qi in range(2)]

        def b1_release_wq(gh):
            for bb, _ in wq_buf[gh]:
                ring_done(bb)

        def b1_load_wo(gh):
            wo_buf[gh] = [ring_load(q_of(wo_d, qi * 512), slot=4 + qi) for qi in range(2)]

        def b1_release_wo(gh):
            for bb, _ in wo_buf[gh]:
                ring_done(bb)

        def emit_proj(gh, oc, qb):
            sl4, sl2 = oc % 3, oc % 2
            buf, bufw = wq_buf[gh][oc // 4]
            for kc in range(KC):
                p.step("tensor",
                       lambda e, qb=qb, kc=kc, oc=oc, buf=buf, gh=gh: e.matmul(
                           pb[qb][:], qa_view(buf)[:, kc, (oc % 4) * 128:(oc % 4) * 128 + 128],
                           xnf[:, kc, gh * T:(gh + 1) * T],
                           start=(kc == 0), stop=False, skip_group_check=True),
                       waits=([bufw, kv_ready] + bank_rd[qb]) if kc == 0 else [])
            bank_rd[qb] = []
            p.step("tensor",
                   lambda e, qb=qb, oc=oc: e.matmul(
                       pb[qb][:], bq_sb[0:1, oc * 128:(oc + 1) * 128], ones_b[0:1, :],
                       start=False, stop=True, skip_group_check=True),
                   waits=[w_const], inc=("sPE", 1))
            qmm = p.after("sPE")
            p.step("scalar",
                   lambda e, qb=qb, sl2=sl2: e.activation(qrel[:, sl2], pb[qb][:], AF.Relu),
                   waits=[qmm] + b1s["qrel_free"][sl2], inc=("sACT", 1))
            b1s["qrel_free"][sl2] = []
            bank_rd[qb] = [p.after("sACT")]
            qre = p.after("sACT")
            p.step("vector",
                   lambda e, gh=gh, sl4=sl4, sl2=sl2: e.tensor_mul(
                       qcs4[:, sl4, 0, :], qrel[:, sl2], costab[:, gh * T:(gh + 1) * T]),
                   waits=[qre, tab_ld] + b1s["qcs_free"][sl4], inc=("sVE", 1))
            b1s["qcs_free"][sl4] = []
            p.step("vector",
                   lambda e, gh=gh, sl4=sl4, sl2=sl2: e.tensor_mul(
                       qcs4[:, sl4, 1, :], qrel[:, sl2], sintab[:, gh * T:(gh + 1) * T]),
                   inc=("sVE", 1))
            m = p.after("sVE")
            b1s["qrel_free"][sl2] = [m]
            qcs_done[(gh, oc)] = m

        def emit_ctxa(gh, oc, cb0, cb1):
            for hh in range(2):
                h = 2 * oc + hh
                b = hh * 64
                cb = (cb0, cb1)[hh]
                nb = 6 + hh
                tp_ctx = (b, b) if b else None
                tp_nrm = (b, 0) if b else None
                for cs in range(2):
                    p.step("tensor",
                           lambda e, s=s, h=h, b=b, cs=cs, cb=cb, tp=tp_ctx, oc=oc, gh=gh: e.matmul(
                               pb[cb][b:b + 64, 0:T],
                               kvsb[b:b + 64, s, h, cs, 0:HD], qcs4[b:b + 64, oc % 3, cs, :],
                               start=(cs == 0), stop=(cs == 1),
                               tile_position=tp, skip_group_check=True),
                           waits=([qcs_done[(gh, oc)]] + bank_rd[cb]) if cs == 0 else [])
                bank_rd[cb] = []
                for cs in range(2):
                    p.step("tensor",
                           lambda e, s=s, h=h, b=b, cs=cs, nb=nb, tp=tp_nrm, oc=oc: e.matmul(
                               pb[nb][0:1, 0:T],
                               kvsb[b:b + 64, s, h, cs, HD:HD + 1], qcs4[b:b + 64, oc % 3, cs, :],
                               start=(cs == 0), stop=(cs == 1),
                               tile_position=tp, skip_group_check=True),
                           waits=(bank_rd[nb] if cs == 0 else []),
                           inc=(("sPE", 1) if cs == 1 else None))
                bank_rd[nb] = []
                cn = p.after("sPE")
                p.step("vector",
                       lambda e, nb=nb, oc=oc, hh=hh: e.tensor_scalar(
                           nrow4[0:1, oc % 2, hh, :], pb[nb][0:1, 0:T], EPS_ATT, None, ALU.add),
                       waits=[cn] + b1s["nrow_free"][oc % 2], inc=("sVE", 1))
                def _recip(e, oc=oc, hh=hh):
                    with nc.allow_low_precision(reason="attn norm reciprocal row stored bf16"):
                        return e.reciprocal(nrow4[0:1, oc % 2, hh, :], nrow4[0:1, oc % 2, hh, :])
                p.step("vector", _recip, waits=[p.after("sVE")], inc=("sVE", 1))
                nrow_done[(gh, oc, hh)] = p.after("sVE")
                bank_rd[nb] = [nrow_done[(gh, oc, hh)]]
                p.step("scalar",
                       lambda e, cb=cb, b=b, oc=oc: e.copy(
                           ctxsb[b:b + 64, oc % 2, :], pb[cb][b:b + 64, 0:T]),
                       waits=[cn] + b1s["ctxsb_free"][oc % 2], inc=("sACT", 1))
                ctx_evac[(gh, oc, hh)] = p.after("sACT")
                bank_rd[cb] = [ctx_evac[(gh, oc, hh)]]
            b1s["ctxsb_free"][oc % 2] = []
            b1s["nrow_free"][oc % 2] = []
            b1s["qcs_free"][oc % 3] = [p.after("sPE")]

        def emit_ctxb(gh, oc, bb0=6, bb1=7):
            for hh in range(2):
                b = hh * 64
                bb = (bb0, bb1)[hh]
                tp_bc = (0, b) if b else None
                p.step("tensor",
                       lambda e, bb=bb, b=b, oc=oc, hh=hh, tp=tp_bc: e.matmul(
                           pb[bb][b:b + 64, 0:T], ones_b[0:1, 0:64], nrow4[0:1, oc % 2, hh, :],
                           start=True, stop=True, tile_position=tp, skip_group_check=True),
                       waits=[nrow_done[(gh, oc, hh)], const_ve] + bank_rd[bb], inc=("sPE", 1))
                bank_rd[bb] = []
                bcm = p.after("sPE")
                p.step("vector",
                       lambda e, bb=bb, b=b, oc=oc: e.tensor_mul(
                           apre[b:b + 64, oc, :], pb[bb][b:b + 64, 0:T], ctxsb[b:b + 64, oc % 2, :]),
                       waits=[bcm, ctx_evac[(gh, oc, hh)]], inc=("sVE", 1))
                bank_rd[bb] = [p.after("sVE")]
            b1s["ctxsb_free"][oc % 2] = [p.after("sVE")]
            b1s["nrow_free"][oc % 2] = [p.after("sPE")]
            apre_all[gh] = p.after("sVE")

        def emit_wout(gh, half, wb0, wb1):
            for j, hc in enumerate(range(half * 4, half * 4 + 4)):
                wb = (wb0, wb1)[j % 2]
                buf, bufw = wo_buf[gh][hc // 4]
                for ec in range(KC):
                    p.step("tensor",
                           lambda e, wb=wb, ec=ec, hc=hc, buf=buf: e.matmul(
                               pb[wb][:], qa_view(buf)[:, ec, (hc % 4) * 128:(hc % 4) * 128 + 128],
                               apre[:, ec, :],
                               start=(ec == 0), stop=False, skip_group_check=True),
                           waits=([apre_all[gh], bufw] + bank_rd[wb]) if ec == 0 else [])
                bank_rd[wb] = []
                p.step("tensor",
                       lambda e, wb=wb, hc=hc: e.matmul(
                           pb[wb][:], bo_sb[0:1, hc * 128:(hc + 1) * 128], ones_b[0:1, :],
                           start=False, stop=True, skip_group_check=True),
                       inc=("sPE", 1))
                omm = p.after("sPE")
                p.step("vector",
                       lambda e, wb=wb, hc=hc, gh=gh: e.tensor_add(
                           acc[:, hc, gh * T:(gh + 1) * T],
                           acc[:, hc, gh * T:(gh + 1) * T], pb[wb][:]),
                       waits=[omm], inc=("sVE", 1))
                bank_rd[wb] = [p.after("sVE")]
            wout_done[gh] = p.after("sVE")

        hook_slices = {}

        def at_hook(h, fn):
            hook_slices.setdefault(h, []).append(fn)
        hook_ctr = [0]

        def run_hook(F):
            h = hook_ctr[0]
            hook_ctr[0] += 1
            for fn in hook_slices.get(h, []):
                fn(F)

        for gh in range(2):
            base = gh * 18
            at_hook(base + 0, lambda F, gh=gh: (
                emit_proj(gh, 0, F[0]), emit_proj(gh, 1, F[1])))
            at_hook(base + 2, lambda F, gh=gh: (
                emit_ctxa(gh, 0, F[2], F[3]), emit_proj(gh, 2, F[0]),
                emit_ctxa(gh, 1, F[2], F[3]), emit_proj(gh, 3, F[1])))
            at_hook(base + 4, lambda F, gh=gh: (
                emit_ctxb(gh, 0), emit_ctxa(gh, 2, F[2], F[3]),
                emit_proj(gh, 4, F[0]), emit_ctxb(gh, 1),
                emit_ctxa(gh, 3, F[2], F[3]), emit_proj(gh, 5, F[1])))
            at_hook(base + 6, lambda F, gh=gh: (
                emit_ctxb(gh, 2), emit_ctxa(gh, 4, F[2], F[3]),
                emit_proj(gh, 6, F[0]), emit_ctxb(gh, 3),
                emit_ctxa(gh, 5, F[2], F[3]), emit_proj(gh, 7, F[1])))
            at_hook(base + 8, lambda F, gh=gh: (
                emit_ctxb(gh, 4), emit_ctxa(gh, 6, F[2], F[3]),
                emit_ctxb(gh, 5), emit_ctxa(gh, 7, F[0], F[1])))
            at_hook(base + 10, lambda F, gh=gh: (
                emit_ctxb(gh, 6), emit_ctxb(gh, 7, F[2], F[3])))
            at_hook(base + 12, lambda F, gh=gh: emit_wout(gh, 0, F[0], F[1]))
            at_hook(base + 14, lambda F, gh=gh: emit_wout(gh, 1, F[2], F[3]))
        at_hook(7, lambda F: b1_release_wq(0))
        at_hook(9, lambda F: b1_load_wo(0))
        at_hook(15, lambda F: b1_release_wo(0))
        at_hook(17, lambda F: b1_load_wq(1))
        at_hook(25, lambda F: b1_release_wq(1))
        at_hook(27, lambda F: b1_load_wo(1))
        at_hook(33, lambda F: b1_release_wo(1))
        if g + 1 < N_GROUP:
            s_nx, gh2_nx = divmod(g + 1, 2)
            t0_nx = gh2_nx * G
            at_hook(33, lambda F, s_nx=s_nx, t0_nx=t0_nx: (
                emit_x_dma(x_in[s_nx, t0_nx:t0_nx + 128, :], 0),
                emit_x_dma(x_in[s_nx, t0_nx + 128:t0_nx + 256, :], 1)))
        b1_load_wq(0)

        gsb_rd = {0: [], 1: []}
        a_ch = {}
        a_wr_gate = [p.after("sPE")]
        ffn_gh = {}
        wbc_ready = {}
        wbc_wr = {0: [], 1: []}

        wbc_wre = {}

        def emit_wbc_a(eidx):
            for half in range(2):
                for s4 in range(4):
                    sub = half * 4 + s4
                    p.step("tensor",
                           lambda e, sub=sub, s4=s4, eidx=eidx: e.matmul(
                               pb[6][0:1, s4 * 128:(s4 + 1) * 128],
                               w_tm[:, sub, eidx:eidx + 1], ident[:],
                               start=True, stop=True, skip_group_check=True),
                           waits=((([wtm_done] if half == 0 else []) + bank_rd[6]) if s4 == 0 else []),
                           inc=(("sPE", 1) if s4 == 3 else None))
                bank_rd[6] = []
                wr_done = p.after("sPE")
                p.step("scalar",
                       lambda e, half=half: e.copy(
                           wrow[0:1, 0, half * 512:(half + 1) * 512], pb[6][0:1, 0:512]),
                       waits=[wr_done], inc=("sACT", 1))
                bank_rd[6] = [p.after("sACT")]
            wbc_wre[eidx] = p.after("sACT")

        def emit_wbc_b(eidx):
            wsl = eidx % 2
            wre = wbc_wre[eidx]
            for gh in range(2):
                p.step("tensor",
                       lambda e, gh=gh: e.matmul(
                           pb[6][:, 0:T], ones_f[:],
                           wrow[0:1, 0, gh * T:(gh + 1) * T],
                           start=True, stop=True, skip_group_check=True),
                       waits=([wre] if gh == 0 else []) + bank_rd[6],
                       inc=("sPE", 1))
                bank_rd[6] = []
                wb = p.after("sPE")
                p.step("scalar",
                       lambda e, gh=gh, wsl=wsl: e.copy(
                           wbc[:, wsl, gh * T:(gh + 1) * T], pb[6][:, 0:T]),
                       waits=[wb] + wbc_wr[wsl], inc=("sACT", 1))
                bank_rd[6] = [p.after("sACT")]
            wbc_ready[eidx] = p.after("sACT")

        def emit_wbc(eidx):
            emit_wbc_a(eidx)
            emit_wbc_b(eidx)

        for ei in range(9):
            routed = ei > 0
            eidx = ei - 1
            gsrc = gw_d[eidx] if routed else sgT
            usrc = uw_d[eidx] if routed else suT
            dsrc = dw_d[eidx] if routed else sdT
            wslot = eidx % 2
            if routed and eidx not in wbc_ready:
                emit_wbc(eidx)
            wbc_done = wbc_ready.get(eidx)

            for ih in range(2):
                for iq in range(2):
                    qi = ih * 2 + iq
                    gb, gld = ring_load(q_of(gsrc, qi * 512))
                    ub, uld = ring_load(q_of(usrc, qi * 512))
                    first_g, first_u = [gld], [uld]
                    for icq in range(4):
                        ic_h = iq * 4 + icq
                        for gh in range(2):
                            un = (qi * 4 + icq) * 2 + gh
                            gbank = un % 2
                            ubank = 2 + un % 2
                            eslot = un % 2
                            for kc in range(KC):
                                p.step("tensor",
                                       lambda e, gbank=gbank, kc=kc, gb=gb, icq=icq, gh=gh: e.matmul(
                                           pb[gbank][:], qa_view(gb)[:, kc, icq * 128:(icq + 1) * 128],
                                           xnf[:, kc, gh * T:(gh + 1) * T],
                                           start=(kc == 0), stop=(kc == KC - 1), skip_group_check=True),
                                       waits=(first_g + bank_rd[gbank]) if kc == 0 else [],
                                       inc=(("sPE", 1) if kc == KC - 1 else None))
                            first_g = []
                            bank_rd[gbank] = []
                            gmm = p.after("sPE")
                            for kc in range(KC):
                                p.step("tensor",
                                       lambda e, ubank=ubank, kc=kc, ub=ub, icq=icq, gh=gh: e.matmul(
                                           pb[ubank][:], qa_view(ub)[:, kc, icq * 128:(icq + 1) * 128],
                                           xnf[:, kc, gh * T:(gh + 1) * T],
                                           start=(kc == 0), stop=(kc == KC - 1), skip_group_check=True),
                                       waits=(first_u + bank_rd[ubank]) if kc == 0 else [],
                                       inc=(("sPE", 1) if kc == KC - 1 else None))
                            first_u = []
                            bank_rd[ubank] = []
                            umm = p.after("sPE")
                            p.step("scalar",
                                   lambda e, gbank=gbank, eslot=eslot: e.activation(
                                       gsb[:, eslot, :], pb[gbank][:], AF.Silu),
                                   waits=[gmm] + gsb_rd[eslot], inc=("sACT", 1))
                            gsb_rd[eslot] = []
                            bank_rd[gbank].append(p.after("sACT"))
                            if routed:
                                p.step("vector",
                                       lambda e, ubank=ubank, eslot=eslot, gh=gh, wslot=wslot: e.tensor_mul(
                                           usb[:, eslot, :], pb[ubank][:],
                                           wbc[:, wslot, gh * T:(gh + 1) * T]),
                                       waits=[umm, wbc_done], inc=("sVE", 1))
                                if routed and un == 31:
                                    wbc_wr[wslot] = [p.after("sVE")]
                            else:
                                p.step("vector",
                                       lambda e, ubank=ubank, eslot=eslot: e.tensor_copy(
                                           usb[:, eslot, :], pb[ubank][:]),
                                       waits=[umm], inc=("sVE", 1))
                            bank_rd[ubank].append(p.after("sVE"))
                            p.step("vector",
                                   lambda e, eslot=eslot, ic_h=ic_h, gh=gh: e.tensor_mul(
                                       a_e[:, ic_h, gh * T:(gh + 1) * T],
                                       gsb[:, eslot, :], usb[:, eslot, :]),
                                   waits=[p.after("sACT")] + a_wr_gate, inc=("sVE", 1))
                            a_wr_gate = []
                            a_ch[(ic_h, gh)] = p.after("sVE")
                            gsb_rd[eslot].append(p.after("sVE"))
                    ring_done(gb)
                    ring_done(ub)
                run_hook([0, 1, 2, 3])
                if ih == 1 and ei + 1 <= 8 and ei in wbc_wre and ei not in wbc_ready:
                    emit_wbc_b(ei)
                dq = []
                for q2 in range(2):
                    r0 = ih * 1024 + q2 * 512
                    dq.append(ring_load(
                        dsrc[r0:r0 + 512, :].rearrange("(c k) h -> k c h", k=128), kdim=4))
                first_d = [dq[0][1], dq[1][1], p.after("sVE")]
                for gh in range(2):
                    for hc in range(KC):
                        dn = gh * KC + hc
                        dbank = 4 + dn % 2
                        for icq in range(KC):
                            buf, bufw = dq[icq // 4]
                            w = []
                            if icq == 0:
                                w = first_d + bank_rd[dbank]
                            p.step("tensor",
                                   lambda e, dbank=dbank, icq=icq, hc=hc, gh=gh, buf=buf: e.matmul(
                                       pb[dbank][:], qd_view(buf)[:, icq % 4, hc * 128:(hc + 1) * 128],
                                       a_e[:, icq, gh * T:(gh + 1) * T],
                                       start=(icq == 0), stop=(icq == KC - 1), skip_group_check=True),
                                   waits=w,
                                   inc=(("sPE", 1) if icq == KC - 1 else None))
                        first_d = []
                        bank_rd[dbank] = []
                        dmm = p.after("sPE")
                        p.step("vector",
                               lambda e, dbank=dbank, hc=hc, gh=gh: e.tensor_add(
                                   acc[:, hc, gh * T:(gh + 1) * T],
                                   acc[:, hc, gh * T:(gh + 1) * T], pb[dbank][:]),
                               waits=[dmm] + ([acc_init_done] if ei == 0 and ih == 0 else []),
                               inc=("sVE", 1))
                        bank_rd[dbank].append(p.after("sVE"))
                    if ei == 8 and ih == 1:
                        ffn_gh[gh] = p.after("sVE")
                for buf, _ in dq:
                    ring_done(buf)
                a_wr_gate = [p.after("sPE")]
                run_hook([4, 5, 2, 3])
                if ih == 0 and ei + 1 <= 8 and ei not in wbc_wre:
                    emit_wbc_a(ei)
        assert hook_ctr[0] == 36 and max(hook_slices) < 36
        ffn_done = p.after("sVE")
        if g == 0:
            tap("acc_g0", acc[:])

        b5_gate = p.after("sACT")
        ostate = {0: [], 1: []}
        for sub in range(N_SUB):
            l0 = t0 + sub * 128
            for rnd in range(4):
                bk = 6 + rnd % 2
                ev = bank_rd[bk]
                for j in range(2):
                    hc = rnd * 2 + j
                    w = []
                    if j == 0:
                        w = [ffn_gh[sub // 4], b5_gate] + ev
                    p.step("tensor",
                           lambda e, hc=hc, j=j, sub=sub, bk=bk: e.transpose(
                               pb[bk][:, j * 128:(j + 1) * 128],
                               acc[:, hc, sub * 128:sub * 128 + 128], ident[:]),
                           waits=w, inc=(("sPE", 1) if j == 1 else None))
                bank_rd[bk] = []
                tp = p.after("sPE")
                p.step("scalar",
                       lambda e, rnd=rnd, bk=bk, osl=sub % 2: e.copy(
                           outst[:, osl, rnd * 256:(rnd + 1) * 256], pb[bk][:, 0:256]),
                       waits=[tp] + ostate[sub % 2], inc=("sACT", 1))
                bank_rd[bk] = [p.after("sACT")]
            ostate[sub % 2] = []
            p.step("scalar",
                   lambda e, s=s, l0=l0, osl=sub % 2: e.dma_start(
                       out_d[s, l0:l0 + 128, :], outst[:, osl]),
                   inc=(f"sO{sub % 2}", 16))
            ostate[sub % 2] = [p.after(f"sO{sub % 2}")]
            out_cnt[0] += 16

    p.step("scalar", None, waits=[p.after("sO0"), p.after("sO1")])

    with nc.Block() as blk:
        for engname in ("sync", "gpsimd", "vector", "scalar", "tensor"):
            def body(eng, engname=engname):
                for (e, fn, waits, inc) in p.steps:
                    if e != engname:
                        continue
                    for (sname, v) in waits:
                        eng.wait_ge(sems[sname], v)
                    if fn is not None:
                        ins = fn(eng)
                        if inc is not None:
                            ins.then_inc(sems[inc[0]], inc[1])
            getattr(blk, engname)(body)

    return nc


_NC_CACHE = [None]


def kernel(L0_attn_x, L0_attn_W_qkv, L0_attn_b_qkv, L0_attn_W_out, L0_attn_b_out,
           L0_ffn_shared_gate, L0_ffn_shared_up, L0_ffn_shared_down,
           L0_ffn_gate_w, L0_ffn_up_w, L0_ffn_down_w,
           L0_ffn_router_w, L0_ffn_routing_bias):
    f32 = lambda a: np.ascontiguousarray(np.asarray(a, dtype=np.float32))
    x = f32(L0_attn_x)

    pos = np.arange(L, dtype=np.float64) / L * (np.pi / 2)
    cosr = np.cos(pos).astype(np.float32)
    sinr = np.sin(pos).astype(np.float32)

    common = {
        "wqkv": f32(L0_attn_W_qkv),
        "bqkv": f32(L0_attn_b_qkv).reshape(1, 3 * D),
        "wo": f32(L0_attn_W_out),
        "bo": f32(L0_attn_b_out).reshape(1, D),
        "sgT": np.ascontiguousarray(f32(L0_ffn_shared_gate).T),
        "suT": np.ascontiguousarray(f32(L0_ffn_shared_up).T),
        "sdT": np.ascontiguousarray(f32(L0_ffn_shared_down).T),
        "gw": f32(L0_ffn_gate_w),
        "uw": f32(L0_ffn_up_w),
        "dw": f32(L0_ffn_down_w),
        "rwT": np.ascontiguousarray(f32(L0_ffn_router_w).T),
        "rbias": f32(L0_ffn_routing_bias).reshape(1, NE),
        "identc": np.eye(128, dtype=np.float32),
        "costab": np.ascontiguousarray(np.broadcast_to(cosr, (128, L))),
        "sintab": np.ascontiguousarray(np.broadcast_to(sinr, (128, L))),
        "coscol": np.ascontiguousarray(cosr.reshape(16, 128).T),
        "sincol": np.ascontiguousarray(sinr.reshape(16, 128).T),
        "idx8": np.ascontiguousarray(np.broadcast_to(np.arange(NE, dtype=np.float32), (128, NE))),
        "idxb8": np.ascontiguousarray(np.broadcast_to(np.arange(NE, dtype=np.float32) + BIG, (128, NE))),
    }

    if _NC_CACHE[0] is None:
        _NC_CACHE[0] = build_program()
    nc = _NC_CACHE[0]

    in_maps = []
    for c in range(N_CORES):
        m = dict(common)
        m["x"] = np.ascontiguousarray(x[c * B_CORE:(c + 1) * B_CORE])
        in_maps.append(m)

    res = run_bass_kernel_spmd(nc, in_maps, core_ids=list(range(N_CORES)))
    return np.concatenate([r["out"] for r in res.results], axis=0)
